# version 02
# baseline (speedup 1.0000x reference)
import numpy as np
import ml_dtypes
import concourse.bass as bass
import concourse.mybir as mybir
import concourse.library_config as libcfg
from concourse.bass_utils import run_bass_kernel_spmd

F32 = mybir.dt.float32
BF16 = mybir.dt.bfloat16
F16 = mybir.dt.float16
I32 = mybir.dt.int32
I16 = mybir.dt.int16
AF = mybir.ActivationFunctionType
ALU = mybir.AluOpType
AX = mybir.AxisListType

S = 2048
D = 1024
NCHUNK = 8
NSEQ = 2
MAGIC = 12582912.0
NEG = -1e9

BLOCKS = ["l0aa", "l0ffn", "l0ab", "l0ffn2", "l1aa", "l1ffn", "l1ab", "l1ffn2"]
BLOCK_SET = list(BLOCKS)
DEV_TAPS = []


def _span(ap):
    t = ap.tensor
    lo = ap.offset * mybir.dt.size(ap.dtype)
    nb = 0
    for st, ct in ap.ap:
        nb += abs(int(st)) * (int(ct) - 1)
    hi = lo + (nb + 1) * mybir.dt.size(ap.dtype)
    return (t.name, lo, hi)


class Tok:
    __slots__ = ("eng", "idx", "dma")

    def __init__(self, eng, idx, dma):
        self.eng, self.idx, self.dma = eng, idx, dma


class Prog:
    ENGS = ("sync", "scalar", "vector", "tensor", "gpsimd")
    QUEUES = ("sync", "scalar", "gpsimd")
    R = 16

    def __init__(self, nc):
        self.nc = nc
        self.items = {e: [] for e in self.ENGS}
        self.seq = 0
        self.csem = {e: nc.alloc_semaphore(f"C{e}") for e in self.ENGS}
        self.dma_list = {q: [] for q in self.QUEUES}
        self.track = {}

    def _dep_waits(self, reads, writes):
        waits = []
        for ap in reads:
            nm, lo, hi = _span(ap)
            rec = self.track.get(nm)
            if rec:
                for (l, h, t) in rec["w"]:
                    if l < hi and lo < h:
                        waits.append(t)
        for ap in writes:
            nm, lo, hi = _span(ap)
            rec = self.track.get(nm)
            if rec:
                for (l, h, t) in rec["w"]:
                    if l < hi and lo < h:
                        waits.append(t)
                for (l, h, t) in rec["r"]:
                    if l < hi and lo < h:
                        waits.append(t)
        return waits

    def _dep_update(self, tok, reads, writes):
        for ap in reads:
            nm, lo, hi = _span(ap)
            rec = self.track.setdefault(nm, {"w": [], "r": []})
            rec["r"].append((lo, hi, tok))
        for ap in writes:
            nm, lo, hi = _span(ap)
            rec = self.track.setdefault(nm, {"w": [], "r": []})
            rec["w"] = [(l, h, t) for (l, h, t) in rec["w"]
                        if not (lo <= l and h <= hi)]
            rec["r"] = [(l, h, t) for (l, h, t) in rec["r"]
                        if not (lo <= l and h <= hi)]
            rec["w"].append((lo, hi, tok))

    def emit(self, eng, fn, reads=(), writes=(), extra=(), dma=False, fence=False):
        waits = self._dep_waits(reads, writes) + [w for w in extra if w is not None]
        it = {"fn": fn, "waits": waits, "inc": dma or fence, "dma": dma,
              "fence": fence, "g": self.seq}
        self.seq += 1
        self.items[eng].append(it)
        idx = len(self.items[eng]) - 1
        it["_self_idx"] = idx
        if dma:
            it["dma_i"] = len(self.dma_list[eng])
            self.dma_list[eng].append(idx)
        tok = Tok(eng, idx, dma)
        self._dep_update(tok, reads, writes)
        return tok

    def dma(self, q, out, in_, extra=(), slow=False):
        return self.emit(q, lambda e: e.dma_start(
            out=out, in_=in_, allow_slow_non_contiguous=slow),
            reads=[in_], writes=[out], extra=extra, dma=True)

    def dmat(self, q, out, in_, extra=()):
        return self.emit(q, lambda e: e.dma_start(out=out, in_=in_, transpose=True),
                         reads=[in_], writes=[out], extra=extra, dma=True)

    def finalize(self):
        self.dring = {q: [self.nc.alloc_semaphore(f"D{q}{j}") for j in range(self.R)]
                      for q in self.QUEUES}
        for e in self.ENGS:
            for it in self.items[e]:
                for t in it["waits"]:
                    if not t.dma:
                        self.items[t.eng][t.idx]["inc"] = True
        self.cval = {e: [0] * len(self.items[e]) for e in self.ENGS}
        for e in self.ENGS:
            c = 0
            for i, it in enumerate(self.items[e]):
                if it["inc"] and not it["dma"]:
                    c += 1
                self.cval[e][i] = c

    def _dma_sem_val(self, t):
        di = self.items[t.eng][t.idx]["dma_i"]
        return self.dring[t.eng][di % self.R], 16 * (di // self.R + 1)

    def replay(self):
        nc = self.nc
        with nc.Block() as block:
            for ename in self.ENGS:
                def body(eng, ename=ename):
                    seen = {}
                    for it in self.items[ename]:
                        resolved = []
                        for t in it["waits"]:
                            if t.dma:
                                sem, val = self._dma_sem_val(t)
                            else:
                                if (t.eng == ename
                                        and not self.items[t.eng][t.idx].get("fence")
                                        and not it.get("fence")):
                                    continue
                                sem, val = self.csem[t.eng], self.cval[t.eng][t.idx]
                            if val > 0 and seen.get(sem.name, -1) < val:
                                resolved.append((sem, val))
                                seen[sem.name] = val
                        for sem, val in resolved:
                            eng.wait_ge(sem, val)
                        if it["dma"]:
                            di = it["dma_i"]
                            if di >= self.R:
                                sem = self.dring[ename][di % self.R]
                                val = 16 * (di // self.R)
                                if seen.get(sem.name, -1) < val:
                                    eng.wait_ge(sem, val)
                                    seen[sem.name] = val
                        ins = it["fn"](eng)
                        if it["dma"]:
                            di = it["dma_i"]
                            ins.then_inc(self.dring[ename][di % self.R], 16)
                        elif it["inc"]:
                            ins.then_inc(self.csem[ename], 1)
                getattr(block, ename)(body)

    def stats(self):
        return {e: len(self.items[e]) for e in self.ENGS}


class Banks:

    def __init__(self, mk):
        self.aps = [mk.ps(f"bank{i}", [128, 512], F32) for i in range(8)]
        self.rr = 0
        self.pinned = set()

    def take(self):
        for _ in range(16):
            i = self.rr % 8
            self.rr += 1
            if i not in self.pinned:
                return self.aps[i]
        raise RuntimeError("all psum banks pinned")

    def take_pin(self):
        for _ in range(16):
            i = self.rr % 8
            self.rr += 1
            if i not in self.pinned:
                self.pinned.add(i)
                return i, self.aps[i]
        raise RuntimeError("all psum banks pinned")

    def unpin(self, i):
        self.pinned.discard(i)


class MK:
    def __init__(self, nc):
        self.nc = nc
        self.p = Prog(nc)
        self.uid = 0

    def name(self, s):
        self.uid += 1
        return f"{s}_{self.uid}"

    def sb(self, nm, shape, dt=F32):
        return self.nc.alloc_sbuf_tensor(self.name(nm), shape, dt).ap()

    def ps(self, nm, shape, dt=F32):
        return self.nc.alloc_psum_tensor(self.name(nm), shape, dt).ap()

    def dram(self, nm, shape, dt=F32, kind=None):
        if kind:
            return self.nc.dram_tensor(nm, shape, dt, kind=kind).ap()
        return self.nc.dram_tensor(self.name(nm), shape, dt).ap()

    def barrier(self, eng, extra):
        if not hasattr(self, "_bar"):
            self._bar = self.sb("bar", [1, 1], F32)
        return self.p.emit(eng, lambda e: e.memset(self._bar[0:1, 0:1], 0.0),
                           writes=[self._bar[0:1, 0:1]], extra=extra)

    def build_constants(self):
        p = self.p
        self.id_f32 = self.sb("idf32", [128, 128], F32)
        self.id_bf16 = self.sb("idbf16", [128, 128], BF16)
        self.iot = self.id_f32.bitcast(I32)
        self.ones_col_b = self.sb("ones_colb", [128, 1], BF16)
        self.ones_col_f = self.sb("ones_colf", [128, 1], F32)
        self.ones_row128 = self.sb("ones_r128", [1, 128], F32)
        p.emit("gpsimd", lambda e: e.iota(self.iot[:, :], [[1, 128]], base=0,
                                          channel_multiplier=-1),
               writes=[self.iot[:, :]])
        p.emit("vector", lambda e: e.tensor_scalar(
            self.id_f32[:, :], self.iot[:, :], 0, None, ALU.is_equal),
            reads=[self.iot[:, :]], writes=[self.id_f32[:, :]])
        p.emit("vector", lambda e: e.tensor_copy(
            self.id_bf16[:, :], self.id_f32[:, :]),
            reads=[self.id_f32[:, :]], writes=[self.id_bf16[:, :]])
        p.emit("vector", lambda e: e.memset(self.ones_col_b[:, :], 1.0),
               writes=[self.ones_col_b[:, :]])
        p.emit("vector", lambda e: e.memset(self.ones_col_f[:, :], 1.0),
               writes=[self.ones_col_f[:, :]])
        p.emit("vector", lambda e: e.memset(self.ones_row128[:, :], 1.0),
               writes=[self.ones_row128[:, :]])
        self.pat_h2 = self.sb("path2", [128, 2], F32)
        self.pat_2t = self.sb("pat2t", [2, 128], F32)
        for ap_, v in ((self.pat_h2[0:64, 0:1], 1.0), (self.pat_h2[64:128, 0:1], 0.0),
                       (self.pat_h2[0:64, 1:2], 0.0), (self.pat_h2[64:128, 1:2], 1.0)):
            p.emit("vector", lambda e, a=ap_, vv=v: e.memset(a, vv), writes=[ap_])

    def tap(self, name, ap, shape=None):
        if name not in DEV_TAPS:
            return
        shape = shape or list(ap.shape)
        if not hasattr(self, "_taps"):
            self._taps = {}
        if name not in self._taps:
            self._taps[name] = self.nc.dram_tensor(
                f"tap_{name}", shape, ap.dtype, kind="ExternalOutput").ap()
        self.p.dma("gpsimd", self._taps[name], ap, slow=True)

    def build_ssd_constants(self, banks, ctx):
        p = self.p
        bc1b = ctx.bc1.bitcast(BF16)
        self.mask_zc = bc1b[0:32, 2480:2512]
        self.mask_ssd = bc1b[:, 2512:2640]
        ari = ctx.ar_a.bitcast(I32)
        zt = ari[0:32, 0:32]
        p.emit("gpsimd", lambda e: e.iota(zt[:, :], [[1, 32]], base=0,
                                          channel_multiplier=-1), writes=[zt[:, :]])
        self.ts("vector", self.mask_zc[:, :], zt[:, :], 0, -1e9,
                ALU.is_le, ALU.mult)
        t1 = ari[:, 0:128]
        t2 = ari[:, 128:256]
        p.emit("gpsimd", lambda e: e.iota(t1[:, :], [[0, 128]], base=0,
                                          channel_multiplier=1), writes=[t1[:, :]])
        p.emit("gpsimd", lambda e: e.iota(t2[:, :], [[1, 128]], base=0,
                                          channel_multiplier=0), writes=[t2[:, :]])
        ca = ari[:, 256:384]
        cb = ari[:, 384:512]
        self.ts("vector", ca[:, :], t1[:, :], 6, None, ALU.logical_shift_right)
        self.ts("vector", cb[:, :], t2[:, :], 6, None, ALU.logical_shift_right)
        meq = t1
        self.tt("vector", meq[:, :], ca[:, :], cb[:, :], ALU.is_equal)
        sm_ = ca
        lm = cb
        p.emit("gpsimd", lambda e: e.iota(sm_[:, :], [[0, 128]], base=0,
                                          channel_multiplier=1), writes=[sm_[:, :]])
        p.emit("gpsimd", lambda e: e.iota(lm[:, :], [[1, 128]], base=0,
                                          channel_multiplier=0), writes=[lm[:, :]])
        self.ts("vector", sm_[:, :], sm_[:, :], 63, None, ALU.bitwise_and)
        self.ts("vector", lm[:, :], lm[:, :], 63, None, ALU.bitwise_and)
        tri = t2
        self.tt("vector", tri[:, :], sm_[:, :], lm[:, :], ALU.is_le)
        keep = ctx.ar_a[:, 512:640]
        self.tt("vector", keep[:, :], meq[:, :], tri[:, :], ALU.mult)
        self.ts("vector", self.mask_ssd[:, :], keep[:, :], -1.0, 1e9,
                ALU.add, ALU.mult)

    def build_l1aa_constants(self, banks, ctx, ins):
        p = self.p
        bc1b = ctx.bc1.bitcast(BF16)
        self.mband = bc1b[:, 3072:3328]
        self.mask0 = None
        self.latbase = bc1b[:, 3328:3392]
        if not hasattr(self, "glpat"):
            self.glpat = self.sb("glpat", [128, 2], F32)
            self.grpat = self.sb("grpat", [128, 2], F32)
        ari = ctx.ar_a.bitcast(I32)
        t = ari[:, 0:256]
        p.emit("gpsimd", lambda e: e.iota(t[:, :], [[1, 256]], base=0,
                                          channel_multiplier=-1), writes=[t[:, :]])
        m1 = ctx.ar_a[:, 256:512]
        m2 = ctx.ar_a[:, 512:768]
        self.ts("vector", m1, t[:, :], 0, -1e9, ALU.is_lt, ALU.mult, fence=True)
        self.ts("vector", m2, t[:, :], 127, -1e9, ALU.is_gt, ALU.mult, fence=True)
        self.tt("vector", self.mband[:, :], m1, m2, ALU.add, fence=True)
        ti = ari[:, 0:64]
        tl = ari[:, 64:128]
        p.emit("gpsimd", lambda e: e.iota(ti[:, :], [[0, 64]], base=0,
                                          channel_multiplier=1), writes=[ti[:, :]])
        p.emit("gpsimd", lambda e: e.iota(tl[:, :], [[1, 64]], base=0,
                                          channel_multiplier=0), writes=[tl[:, :]])
        self.ts("vector", ti[:, :], ti[:, :], 5, None, ALU.logical_shift_right,
                fence=True)
        self.tt("vector", self.latbase[:, :], tl[:, :], ti[:, :], ALU.subtract,
                fence=True)
        for pat, src in ((self.glpat, ins["L1_attn_a_gl_w"]),
                         (self.grpat, ins["L1_attn_a_gr_w"])):
            self.memset("vector", pat[:, :], 0.0, fence=True)
            p.dma("gpsimd", pat[0:64, 0:1], src[:, :], slow=True)
            p.dma("gpsimd", pat[64:128, 1:2], src[:, :], slow=True)

    def build_l0ab_constants(self, banks, ctx):
        p = self.p
        self.sel_e = self.sb("sel_e", [16, 8], F32)
        self.sel_o = self.sb("sel_o", [16, 8], F32)
        ari = ctx.ar_a.bitcast(I32)
        t = ari[0:16, 0:8]
        p.emit("gpsimd", lambda e: e.iota(t[:, :], [[-2, 8]], base=0,
                                          channel_multiplier=1), writes=[t[:, :]])
        self.ts("vector", self.sel_e[:, :], t[:, :], 0, None, ALU.is_equal,
                fence=True)
        self.ts("vector", self.sel_o[:, :], t[:, :], 1, None, ALU.is_equal,
                fence=True)

    def finish_constants(self, banks):
        bank = banks.take()
        self.p.emit("tensor", lambda e: e.transpose(bank[0:2, 0:128], self.pat_h2[:, :],
                                                    self.id_f32[:, :]),
                    reads=[self.pat_h2[:, :], self.id_f32[:, :]],
                    writes=[bank[0:2, 0:128]])
        self.p.emit("vector", lambda e: e.tensor_copy(self.pat_2t[:, :], bank[0:2, 0:128]),
                    reads=[bank[0:2, 0:128]], writes=[self.pat_2t[:, :]])

    def pe_transpose(self, out_psum, in_sbuf, bf16=False):
        k = int(in_sbuf.ap[0][1])
        idn = (self.id_bf16 if bf16 else self.id_f32)[0:k, 0:k]
        return self.p.emit("tensor", lambda e: e.transpose(out_psum, in_sbuf, idn),
                           reads=[in_sbuf, idn], writes=[out_psum])

    def mm(self, out, lhsT, rhs, start, stop, tp=None):
        return self.p.emit("tensor", lambda e: e.matmul(
            out, lhsT, rhs, start=start, stop=stop, tile_position=tp),
            reads=[lhsT, rhs] + ([] if start else [out]), writes=[out])

    def tt(self, eng, out, a, b, op, fence=False):
        return self.p.emit(eng, lambda e: e.tensor_tensor(out, a, b, op),
                           reads=[a, b], writes=[out], fence=fence)

    def ts(self, eng, out, in0, s1, s2, op0, op1=ALU.bypass, fence=False):
        rd = [in0] + [x for x in (s1, s2) if hasattr(x, "tensor")]
        return self.p.emit(eng, lambda e: e.tensor_scalar(out, in0, s1, s2, op0, op1),
                           reads=rd, writes=[out], fence=fence)

    def stt(self, eng, out, in0, scalar, in1, op0, op1):
        rd = [in0, in1] + ([scalar] if hasattr(scalar, "tensor") else [])
        return self.p.emit(eng, lambda e: e.scalar_tensor_tensor(
            out, in0, scalar, in1, op0, op1), reads=rd, writes=[out])

    def act(self, out, in_, func, bias=0.0, scale=1.0):
        rd = [in_] + [x for x in (bias, scale) if hasattr(x, "tensor")]
        return self.p.emit("scalar", lambda e: e.activation(
            out=out, in_=in_, func=func, bias=bias, scale=scale),
            reads=rd, writes=[out])

    def cp(self, eng, out, in_, fence=False):
        return self.p.emit(eng, lambda e: e.tensor_copy(out, in_),
                           reads=[in_], writes=[out], fence=fence)

    def memset(self, eng, ap, v, fence=False):
        return self.p.emit(eng, lambda e: e.memset(ap, v), writes=[ap], fence=fence)

    def red(self, out, in_, op, absval=False, negate=False):
        return self.p.emit("vector", lambda e: e.tensor_reduce(
            out, in_, AX.X, op, apply_absolute_value=absval, negate=negate),
            reads=[in_], writes=[out], fence=True)

    def recip(self, out, in_):
        return self.p.emit("vector", lambda e: e.reciprocal(out, in_),
                           reads=[in_], writes=[out], fence=True)


class Smalls:

    def __init__(self, mk, ncol=136):
        self.mk = mk
        self.buf = mk.sb("smalls", [128, ncol], F32)
        self.cols = {}
        self.next = 0

    def load(self, name, dram_ap, n, q="gpsimd"):
        ncols = max(1, (n + 127) // 128)
        c0 = self.next
        self.next += ncols
        self.cols[name] = (c0, n)
        if n >= 128:
            assert n % 128 == 0
            src = dram_ap.rearrange("(c p) -> p c", p=128)
            self.mk.p.dma(q, self.buf[:, c0:c0 + ncols], src, slow=True)
        else:
            src = dram_ap.rearrange("(p c) -> p c", c=1)
            self.mk.p.dma(q, self.buf[0:n, c0:c0 + 1], src, slow=True)
        return c0

    def col(self, name, chunk=0, rows=128):
        c0, n = self.cols[name]
        return self.buf[0:rows, c0 + chunk:c0 + chunk + 1]


class SeqCtx:
    def __init__(self, mk, banks):
        self.mk = mk
        self.banks = banks
        self.x_hi = mk.sb("x_hi", [128, NCHUNK * S], BF16)
        self.x_lo = mk.sb("x_lo", [128, NCHUNK * S], BF16)
        self.ar_a = mk.sb("ar_a", [128, NCHUNK * S], F32)
        self.ar_b = mk.sb("ar_b", [128, 2048], F32)
        self.ar_r = mk.sb("ar_r", [128, 2048], F32)
        self.ar_c = mk.sb("ar_c", [128, S], F32)
        self.kx = mk.sb("kx", [128, NCHUNK * S], BF16)
        self.bc0 = mk.sb("bc0", [128, S], F32)
        self.bc1 = mk.sb("bc1", [128, S], F32)
        self.wbuf = mk.sb("wbuf", [128, 2048], BF16)
        self.pm = mk.sb("pm", [128, 32], F32)
        self.csrow = mk.sb("csrow", [1, 128], F32)

    def xh(self, fc, t0=0, t1=S):
        return self.x_hi[:, fc * S + t0: fc * S + t1]

    def xl(self, fc, t0=0, t1=S):
        return self.x_lo[:, fc * S + t0: fc * S + t1]

    def arow(self, i):
        return self.ar_a[0:1, i * S:(i + 1) * S]


def rank1_bc(mk, ctx, dst_bc, row_src):
    for t4 in range(4):
        bank = ctx.banks.take()
        sl = slice(t4 * 512, (t4 + 1) * 512)
        mk.mm(bank[:, 0:512], mk.ones_row128[:, :], row_src[0:1, sl], True, True)
        mk.cp("vector", dst_bc[:, sl], bank[:, 0:512])


def ln_stats(mk, ctx, mu_row, negmu_row, r_row):
    sxb = [ctx.banks.take() for _ in range(4)]
    sqb = [ctx.banks.take() for _ in range(4)]
    sqh = ctx.kx[:, 0:S]
    sql = ctx.kx[:, S:2 * S]
    sqf = ctx.ar_b[:, 0:S]
    for fc in range(NCHUNK):
        mk.tt("vector", ctx.ar_c[:, 0:S], ctx.xh(fc), ctx.xl(fc), ALU.add)
        mk.act(sqf, ctx.ar_c[:, 0:S], AF.Square)
        mk.act(sqh, sqf, AF.Copy)
        mk.tt("vector", sql, sqf, sqh, ALU.subtract)
        for t4 in range(4):
            for bk, srcs in ((sxb[t4], (ctx.xh(fc, t4 * 512, t4 * 512 + 512),
                                        ctx.xl(fc, t4 * 512, t4 * 512 + 512))),
                             (sqb[t4], (sqh[:, t4 * 512:t4 * 512 + 512],
                                        sql[:, t4 * 512:t4 * 512 + 512]))):
                for jj, srap in enumerate(srcs):
                    mk.mm(bk[0:1, 0:512], mk.ones_col_b[:, :], srap,
                          fc == 0 and jj == 0, fc == NCHUNK - 1 and jj == 1)
    for t4 in range(4):
        sl = slice(t4 * 512, (t4 + 1) * 512)
        mk.ts("vector", mu_row[0:1, sl], sxb[t4][0:1, 0:512], 1.0 / D, None, ALU.mult)
        if negmu_row is not None:
            mk.ts("vector", negmu_row[0:1, sl], sxb[t4][0:1, 0:512], -1.0 / D, None,
                  ALU.mult)
        mk.tt("vector", r_row[0:1, sl], mu_row[0:1, sl], mu_row[0:1, sl], ALU.mult)
        mk.stt("vector", r_row[0:1, sl], sqb[t4][0:1, 0:512], 1.0 / D,
               r_row[0:1, sl], ALU.mult, ALU.subtract)
        mk.ts("vector", r_row[0:1, sl], r_row[0:1, sl], 1e-5, None, ALU.add)
        mk.recip(r_row[0:1, sl], r_row[0:1, sl])
        mk.act(r_row[0:1, sl], r_row[0:1, sl], AF.Sqrt)


def pm_from_row(mk, ctx, dst_pm, row_ap):
    hb = mk.dram("pmbounce", [2048], F32)
    mk.p.dma("gpsimd", hb.rearrange("(o t) -> o t", o=1), row_ap)
    mk.p.dma("gpsimd", dst_pm, hb.rearrange("(c p) -> p c", p=128), slow=True)


def row_from_pm(mk, ctx, dst_row, pm_ap):
    hb = mk.dram("rowbounce", [2048], F32)
    mk.p.dma("gpsimd", hb.rearrange("(c p) -> p c", p=128), pm_ap, slow=True)
    mk.p.dma("gpsimd", dst_row, hb.rearrange("(o t) -> o t", o=1))


class Wts:
    pass


def prep_ternary_l0(mk, W, ctx, packed_in):
    p = mk.p
    nat = mk.dram("l0f_nat", [D, D], BF16)
    wt = mk.dram("l0f_wt", [8, 128, D], BF16)
    buf = ctx.ar_b[:, :].bitcast(I32)
    obuf = ctx.kx[:, 0:D]
    pk = packed_in.rearrange("(r i) -> r i", i=D)
    for b in range(2):
        p.dma("sync", buf[:, 0:D], pk[b * 128:(b + 1) * 128, :])
        for j in range(4):
            mk.ts("vector", buf[:, D:2 * D], buf[:, 0:D], 2 * j, 3,
                  ALU.logical_shift_right, ALU.bitwise_and)
            mk.ts("vector", obuf[:, :], buf[:, D:2 * D], -1, None, ALU.add)
            p.dma("scalar", nat[256 * j + 128 * b: 256 * j + 128 * b + 128, :],
                  obuf[:, :])
    tbuf = ctx.kx[:, D:2 * D]
    for ic in range(8):
        p.dmat("sync", tbuf[:, 0:D], nat[:, ic * 128:(ic + 1) * 128])
        for oc in range(8):
            p.dma("scalar", wt[oc, :, ic * 128:(ic + 1) * 128],
                  tbuf[:, oc * 128:(oc + 1) * 128])
    W.l0f_wt = wt


def prep_wt(mk, ctx, name, w_in, O, I, natural_oi=True, want_colsum=False):
    p = mk.p
    noc, nkc = (O + 127) // 128, I // 128
    slab_h = mk.dram(f"{name}_h", [noc, 128, I], BF16)
    slab_l = mk.dram(f"{name}_l", [noc, 128, I], BF16)
    cs = mk.dram(f"{name}_cs", [O], F32) if want_colsum else None
    fbuf = ctx.ar_a[:, 0:4096]
    hbuf = ctx.kx[:, 0:4096]
    lbuf = ctx.kx[:, 4096:8192]
    csbuf = ctx.ar_c[:, 0:16]
    if natural_oi:
        nat_h = mk.dram(f"{name}_nh", [O, I], BF16)
        nat_l = mk.dram(f"{name}_nl", [O, I], BF16)
        for rc in range(noc):
            nr = min(128, O - rc * 128)
            p.dma("sync", fbuf[0:nr, 0:I], w_in[rc * 128:rc * 128 + nr, :])
            mk.act(hbuf[0:nr, 0:I], fbuf[0:nr, 0:I], AF.Copy)
            mk.tt("vector", lbuf[0:nr, 0:I], fbuf[0:nr, 0:I], hbuf[0:nr, 0:I],
                  ALU.subtract)
            p.dma("scalar", nat_h[rc * 128:rc * 128 + nr, :], hbuf[0:nr, 0:I])
            p.dma("scalar", nat_l[rc * 128:rc * 128 + nr, :], lbuf[0:nr, 0:I])
            if want_colsum:
                mk.red(csbuf[0:nr, rc % 16:rc % 16 + 1], fbuf[0:nr, 0:I], ALU.add)
                p.dma("gpsimd", cs[rc * 128:rc * 128 + nr].rearrange(
                    "(p o) -> p o", o=1), csbuf[0:nr, rc % 16:rc % 16 + 1], slow=True)
        for nat, slab in ((nat_h, slab_h), (nat_l, slab_l)):
            for kc in range(nkc):
                p.dmat("sync", hbuf[:, 0:O], nat[:, kc * 128:(kc + 1) * 128])
                for oc in range(noc):
                    nr = min(128, O - oc * 128)
                    p.dma("scalar", slab[oc, :, kc * 128:kc * 128 + nr],
                          hbuf[:, oc * 128:oc * 128 + nr])
        if O % 128:
            nr0 = O % 128
            zb = lbuf[:, 0:128]
            mk.memset("vector", zb, 0.0)
            for slab in (slab_h, slab_l):
                for kc in range(nkc):
                    p.dma("scalar", slab[noc - 1, :, kc * 128 + nr0:(kc + 1) * 128],
                          zb[:, 0:128 - nr0])
    else:
        for kc in range(nkc):
            p.dma("sync", fbuf[:, 0:O], w_in[kc * 128:(kc + 1) * 128, :])
            mk.act(hbuf[:, 0:O], fbuf[:, 0:O], AF.Copy)
            mk.tt("vector", lbuf[:, 0:O], fbuf[:, 0:O], hbuf[:, 0:O], ALU.subtract)
            for oc in range(noc):
                p.dma("scalar", slab_h[oc, :, kc * 128:(kc + 1) * 128],
                      hbuf[:, oc * 128:(oc + 1) * 128])
                p.dma("scalar", slab_l[oc, :, kc * 128:(kc + 1) * 128],
                      lbuf[:, oc * 128:(oc + 1) * 128])
        if want_colsum:
            for ot in range((O + 511) // 512):
                w = min(512, O - ot * 512)
                bank = ctx.banks.take()
                for kc in range(nkc):
                    p.dma("sync", fbuf[:, 0:w],
                          w_in[kc * 128:(kc + 1) * 128, ot * 512:ot * 512 + w])
                    mk.mm(bank[0:1, 0:w], mk.ones_col_f[:, :], fbuf[:, 0:w],
                          kc == 0, kc == nkc - 1)
                mk.cp("vector", ctx.ar_c[0:1, 0:w], bank[0:1, 0:w])
                p.dma("gpsimd", cs[ot * 512:ot * 512 + w].rearrange(
                    "(o t) -> o t", o=1), ctx.ar_c[0:1, 0:w])
    return slab_h, slab_l, cs


def prep_quant_l1(mk, ctx, W, ins):
    p = mk.p
    W.qinv_col = mk.sb("qinvc", [128, 4], F32)
    W.qinv_row = mk.sb("qinvr", [1, 4], F32)
    fbuf = ctx.ar_a[:, 0:4096]
    hbuf = ctx.kx[:, 0:4096]
    specs = [("wg", ins["L1_ffn_w_gate"], 4096, 1024, "oc"),
             ("wu", ins["L1_ffn_w_up"], 4096, 1024, "oc"),
             ("wd", ins["L1_ffn_w_down"], 1024, 4096, "kc")]
    for j, (name, w_in, O, I, order) in enumerate(specs):
        noc, nkc = O // 128, I // 128
        acc = ctx.ar_b[:, 0:1]
        for rc in range(noc):
            p.dma("sync", fbuf[:, 0:I], w_in[rc * 128:(rc + 1) * 128, :])
            if rc == 0:
                mk.red(acc, fbuf[:, 0:I], ALU.add, absval=True)
            else:
                mk.red(ctx.ar_b[:, 1:2], fbuf[:, 0:I], ALU.add, absval=True)
                mk.tt("vector", acc, acc, ctx.ar_b[:, 1:2], ALU.add)
        bank = ctx.banks.take()
        mk.mm(bank[0:1, 0:1], mk.ones_col_f[:, :], acc, True, True)
        mk.ts("vector", W.qinv_row[0:1, j:j + 1], bank[0:1, 0:1], 1.0 / (O * I),
              1e-5, ALU.mult, ALU.max)
        mk.recip(ctx.ar_b[0:1, 2:3], W.qinv_row[0:1, j:j + 1])
        b2 = ctx.banks.take()
        mk.mm(b2[:, 0:1], mk.ones_row128[:, :], ctx.ar_b[0:1, 2:3], True, True)
        mk.cp("vector", ctx.ar_b[:, 3:4], b2[:, 0:1])
        b3 = ctx.banks.take()
        mk.mm(b3[:, 0:1], mk.ones_row128[:, :], W.qinv_row[0:1, j:j + 1], True, True)
        mk.cp("vector", W.qinv_col[:, j:j + 1], b3[:, 0:1])
        nat = mk.dram(f"{name}_nat", [O, I], BF16)
        for rc in range(noc):
            p.dma("sync", fbuf[:, 0:I], w_in[rc * 128:(rc + 1) * 128, :])
            mk.ts("vector", fbuf[:, 0:I], fbuf[:, 0:I], ctx.ar_b[:, 3:4], MAGIC,
                  ALU.mult, ALU.add)
            mk.ts("vector", fbuf[:, 0:I], fbuf[:, 0:I], -MAGIC, None, ALU.add)
            mk.ts("vector", hbuf[:, 0:I], fbuf[:, 0:I], 1.0, -1.0, ALU.min, ALU.max)
            p.dma("scalar", nat[rc * 128:(rc + 1) * 128, :], hbuf[:, 0:I])
        slab = mk.dram(f"{name}_wt", [nkc if order == "kc" else noc, 128,
                                      O if order == "kc" else I], BF16)
        tb = ctx.kx[:, 4096:4096 + 4096]
        for kc in range(nkc):
            p.dmat("sync", tb[:, 0:O], nat[:, kc * 128:(kc + 1) * 128])
            if order == "kc":
                p.dma("scalar", slab[kc, :, :], tb[:, 0:O])
            else:
                for oc in range(noc):
                    p.dma("scalar", slab[oc, :, kc * 128:(kc + 1) * 128],
                          tb[:, oc * 128:(oc + 1) * 128])
        setattr(W, name, slab)


def emit_prep(mk, W, ctx, ins):
    prep_ternary_l0(mk, W, ctx, ins["L0_ffn_packed_weight"])
    if "l0aa" in BLOCK_SET:
        W.qkv_h, W.qkv_l, W.qkv_cs = prep_wt(
            mk, ctx, "wqkv", ins["L0_attn_a_Wqkv"], 3072, D, True, True)
        W.proj_h, W.proj_l, _ = prep_wt(
            mk, ctx, "wproj", ins["L0_attn_a_Wproj"], D, D, True, False)
    if "l1ffn" in BLOCK_SET or "l1ffn2" in BLOCK_SET:
        prep_quant_l1(mk, ctx, W, ins)
    if "l1ab" in BLOCK_SET:
        W.inp_h, W.inp_l, W.inp_cs = prep_wt(
            mk, ctx, "winp", ins["L1_attn_b_in_proj_w"], 1040, D, True, True)
        W.outp_h, W.outp_l, _ = prep_wt(
            mk, ctx, "woutp", ins["L1_attn_b_out_proj_w"], D, D, True, False)
    if "l0ab" in BLOCK_SET:
        W.wv_h, W.wv_l, W.wv_cs = prep_wt(
            mk, ctx, "wwv", ins["L0_attn_b_wave_w"], 16, D, True, True)
        W.kn_h, W.kn_l, W.kn_cs = prep_wt(
            mk, ctx, "wkn", ins["L0_attn_b_kernel_w"], 512, D, True, True)
        W.ow_h, W.ow_l, _ = prep_wt(
            mk, ctx, "wow", ins["L0_attn_b_out_w"], D, D, True, False)
    if "l1aa" in BLOCK_SET:
        W.aqkv_h, W.aqkv_l, W.aqkv_cs = prep_wt(
            mk, ctx, "waqkv", ins["L1_attn_a_qkv_w"], 3072, D, False, True)
        W.arq_h, W.arq_l, W.arq_cs = prep_wt(
            mk, ctx, "warq", ins["L1_attn_a_rq_w"], 256, D, False, True)
        W.arout_h, W.arout_l, _ = prep_wt(
            mk, ctx, "warout", ins["L1_attn_a_rout_w"], D, 256, False, False)
        W.aout_h, W.aout_l, _ = prep_wt(
            mk, ctx, "waout", ins["L1_attn_a_out_w"], D, D, False, False)


def emit_l0_ffn(mk, ctx, W, sm, second):
    banks = ctx.banks
    p = mk.p
    rowA = ctx.ar_r[0:1, 0:S]
    rowB = ctx.ar_b[0:1, 0:S]
    ln_stats(mk, ctx, rowA, None, rowB)
    rank1_bc(mk, ctx, ctx.bc0, rowA)
    acc = ctx.ar_c[:, 0:S]
    accn = ctx.kx[:, 0:2 * S].bitcast(F32)
    for fc in range(NCHUNK):
        dv = ctx.ar_a[:, fc * S:(fc + 1) * S]
        mk.tt("vector", dv, ctx.xh(fc), ctx.xl(fc), ALU.add)
        mk.tt("vector", dv, dv, ctx.bc0[:, :], ALU.subtract)
        if fc == 0:
            mk.cp("vector", acc, dv)
            mk.ts("vector", accn, dv, -1.0, None, ALU.mult)
        else:
            mk.tt("vector", acc, acc, dv, ALU.max)
            mk.stt("vector", accn, dv, -1.0, accn, ALU.mult, ALU.max)
    mk.tt("vector", acc, acc, accn, ALU.max)
    amax_pm = ctx.ar_r[:, 2032:2048]
    for tc in range(16):
        bank = banks.take()
        mk.pe_transpose(bank[:, 0:128], acc[:, tc * 128:(tc + 1) * 128])
        mk.red(amax_pm[:, tc:tc + 1], bank[:, 0:128], ALU.max)
    row_from_pm(mk, ctx, rowA, amax_pm)
    mk.tt("vector", rowA, rowA, rowB, ALU.mult)
    mk.ts("vector", rowA, rowA, 1e-5, None, ALU.max)
    mk.recip(rowA, rowA)
    mk.ts("vector", rowA, rowA, 127.0, None, ALU.mult)
    mk.tt("vector", rowA, rowA, rowB, ALU.mult)
    rank1_bc(mk, ctx, ctx.bc1, rowA)
    mk.recip(rowA, rowA)
    mk.tt("vector", rowA, rowA, rowB, ALU.mult)
    mk.ts("vector", rowA, rowA, sm.col("l0f_ws")[0:1, :], None, ALU.mult)
    rank1_bc(mk, ctx, ctx.bc0, rowA)
    for fc in range(NCHUNK):
        dv = ctx.ar_a[:, fc * S:(fc + 1) * S]
        mk.tt("vector", ctx.ar_c[:, 0:S], dv, ctx.bc1[:, :], ALU.mult)
        mk.ts("vector", ctx.ar_c[:, 0:S], ctx.ar_c[:, 0:S], MAGIC, -MAGIC,
              ALU.add, ALU.add)
        mk.ts("vector", ctx.kx[:, fc * S:(fc + 1) * S], ctx.ar_c[:, 0:S],
              127.0, -128.0, ALU.min, ALU.max)
    for oc in range(8):
        wsl = ctx.wbuf[:, (oc % 2) * D:(oc % 2) * D + D]
        p.dma(("sync", "scalar")[oc % 2], wsl, W.l0f_wt[oc, :, :])
        for t4 in range(4):
            bank = banks.take()
            tsl = slice(t4 * 512, (t4 + 1) * 512)
            for kc in range(8):
                mk.mm(bank[:, 0:512],
                      wsl[:, kc * 128:(kc + 1) * 128],
                      ctx.kx[:, kc * S + t4 * 512: t4 * 512 + kc * S + 512],
                      kc == 0, kc == 7)
            t0 = ctx.ar_c[:, 0:512]
            mk.tt("vector", t0, bank[:, 0:512], ctx.bc0[:, tsl], ALU.mult)
            mk.stt("vector", t0, ctx.xh(oc, t4 * 512, t4 * 512 + 512),
                   sm.col("l0f_b", oc), t0, ALU.add, ALU.add)
            mk.tt("vector", t0, t0, ctx.xl(oc, t4 * 512, t4 * 512 + 512), ALU.add)
            mk.act(ctx.xh(oc, t4 * 512, t4 * 512 + 512), t0, AF.Copy)
            mk.tt("vector", ctx.xl(oc, t4 * 512, t4 * 512 + 512), t0,
                  ctx.xh(oc, t4 * 512, t4 * 512 + 512), ALU.subtract)


def emit_l0_attn_a(mk, ctx, W, sm, ins):
    p = mk.p
    banks = ctx.banks
    kxb = ctx.kx
    kxf = ctx.kx.bitcast(F32)
    arab = ctx.ar_a.bitcast(BF16)
    QH, QL, KH, KL, VH, VL, FSC = 0, 2048, 4096, 6144, 8192, 10240, 6144

    negmu = ctx.ar_r[0:1, 0:S]
    r_row = ctx.ar_b[0:1, 0:S]
    mu_row = ctx.ar_c[0:1, 0:S]
    ln_stats(mk, ctx, mu_row, negmu, r_row)
    rank1_bc(mk, ctx, ctx.bc0, r_row)
    pm_from_row(mk, ctx, ctx.pm[:, 0:16], r_row)

    def load_cs(cs_dram, off, slot):
        dst = ctx.csrow[0:1, 0:128]
        p.dma("gpsimd", dst, cs_dram[off:off + 128].rearrange("(o t) -> o t", o=1))
        return dst

    def fold_mm_fm(slab_h, slab_l, cs_dram, oc, dst_cb, wslot):
        wh = ctx.wbuf[:, wslot * 1024:(wslot + 1) * 1024]
        wl_t = mk.sb("wlo_t", [128, 1024], BF16) if False else None
        p.dma("sync", ctx.wbuf[:, 0:1024], slab_h[oc, :, :])
        p.dma("scalar", ctx.wbuf[:, 1024:2048], slab_l[oc, :, :])
        cs = load_cs(cs_dram, oc * 128, 0)
        for t4 in range(4):
            bank = banks.take()
            t0, t1 = t4 * 512, t4 * 512 + 512
            first = True
            for kc in range(NCHUNK):
                wh_t = ctx.wbuf[:, kc * 128:(kc + 1) * 128]
                wl = ctx.wbuf[:, 1024 + kc * 128:1024 + kc * 128 + 128]
                mk.mm(bank[:, 0:512], wh_t, ctx.xh(kc, t0, t1), first, False)
                first = False
                mk.mm(bank[:, 0:512], wh_t, ctx.xl(kc, t0, t1), False, False)
                mk.mm(bank[:, 0:512], wl, ctx.xh(kc, t0, t1), False, False)
            mk.mm(bank[:, 0:512], cs, negmu[0:1, t0:t1], False, True)
            dst_cb(t4, bank)
        return

    for g in range(8):
        qraw = kxf[:, FSC:FSC + 2048]
        kraw = ctx.ar_a[:, g * 2048:(g + 1) * 2048]
        def mk_qk(dst, bias_cols, oc):
            def cb(t4, bank, dst=dst, bias_cols=bias_cols):
                t0, t1 = t4 * 512, t4 * 512 + 512
                mk.tt("vector", dst[:, t0:t1], bank[:, 0:512], ctx.bc0[:, t0:t1],
                      ALU.mult)
                mk.ts("vector", dst[:, t0:t1], dst[:, t0:t1], bias_cols, None,
                      ALU.add)
            fold_mm_fm(W.qkv_h, W.qkv_l, W.qkv_cs, oc, cb, 0)
        mk_qk(qraw, sm.col("bqkv", g), g)
        mk_qk(kraw, sm.col("bqkv", 8 + g), 8 + g)
        for which, raw, scale_extra, dsth, dstl in (
                ("q", qraw, sm.col("qnkn")[:, 0:1], kxb[:, QH:QH + S], kxb[:, QL:QL + S]),
                ("k", kraw, None, kxb[:, KH:KH + S], kxb[:, KL:KL + S])):
            sq = ctx.ar_c[:, 0:S]
            mk.act(sq, raw, AF.Square)
            rb = [banks.take() for _ in range(4)]
            for t4 in range(4):
                mk.mm(rb[t4][0:2, 0:512], mk.pat_h2[:, :],
                      sq[:, t4 * 512:(t4 + 1) * 512], True, True)
            rows = ctx.ar_b[0:2, 0:S]
            for t4 in range(4):
                sl = slice(t4 * 512, (t4 + 1) * 512)
                mk.ts("vector", rows[:, sl], rb[t4][0:2, 0:512], 1.0 / 64, 1e-6,
                      ALU.mult, ALU.add)
                mk.recip(rows[:, sl], rows[:, sl])
                mk.act(rows[:, sl], rows[:, sl], AF.Sqrt)
            for t4 in range(4):
                bank = banks.take()
                sl = slice(t4 * 512, (t4 + 1) * 512)
                mk.mm(bank[:, 0:512], mk.pat_2t[0:2, :], rows[:, sl], True, True)
                mk.cp("vector", ctx.bc1[:, sl], bank[:, 0:512])
            if scale_extra is not None:
                mk.stt("vector", raw, raw, scale_extra, ctx.bc1[:, :], ALU.mult,
                       ALU.mult)
            else:
                mk.tt("vector", raw, raw, ctx.bc1[:, :], ALU.mult)
            mk.act(dsth, raw, AF.Copy)
            mk.tt("vector", dstl, raw, dsth, ALU.subtract)
            mk.tap(f"{which}{g}", raw)
        vfm = ctx.ar_a[:, g * 2048:(g + 1) * 2048]
        vrow = ins["L0_attn_a_bqkv"][2048 + g * 128: 2048 + g * 128 + 128]
        vrow_sb = ctx.ar_c[0:1, 384:512]
        p.dma("gpsimd", vrow_sb, vrow.rearrange("(o t) -> o t", o=1))
        vb_bank = banks.take()
        mk.mm(vb_bank[:, 0:128], mk.ones_row128[:, :], vrow_sb, True, True)
        vbias = ctx.ar_c[:, 0:128]
        mk.cp("vector", vbias, vb_bank[:, 0:128])
        def cb_v(t4, bank):
            t0, t1 = t4 * 512, t4 * 512 + 512
            mk.tt("vector", vfm[:, t0:t1], bank[:, 0:512], ctx.bc0[:, t0:t1],
                  ALU.mult)
        fold_mm_fm(W.qkv_h, W.qkv_l, W.qkv_cs, 16 + g, cb_v, 0)
        for tc in range(16):
            bank = banks.take()
            mk.pe_transpose(bank[:, 0:128], vfm[:, tc * 128:(tc + 1) * 128])
            vt = ctx.ar_c[:, 128:256]
            mk.tt("vector", vt, bank[:, 0:128], vbias, ALU.add)
            mk.act(kxb[:, VH + tc * 128:VH + tc * 128 + 128], vt, AF.Copy)
            mk.tt("vector", kxb[:, VL + tc * 128:VL + tc * 128 + 128], vt,
                  kxb[:, VH + tc * 128:VH + tc * 128 + 128], ALU.subtract)
        mk.tap(f"v{g}", kxb[:, VH:VH + S])
        mk.tap(f"vl{g}", kxb[:, VL:VL + S])
        arcb = ctx.ar_c.bitcast(BF16)
        for grp in range(8):
            avi, avb = banks.take_pin()
            for h2 in range(2):
                hp = slice(h2 * 64, h2 * 64 + 64)
                tp = (h2 * 64, 0)
                for sub in range(2):
                    q0 = grp * 256 + sub * 128
                    sb = [banks.take() for _ in range(4)]
                    for kt in range(4):
                        k0 = kt * 512
                        mk.mm(sb[kt][:, 0:512], kxb[hp, QH + q0:QH + q0 + 128],
                              kxb[hp, KH + k0:KH + k0 + 512], True, False, tp=tp)
                        mk.mm(sb[kt][:, 0:512], kxb[hp, QL + q0:QL + q0 + 128],
                              kxb[hp, KH + k0:KH + k0 + 512], False, False, tp=tp)
                        mk.mm(sb[kt][:, 0:512], kxb[hp, QH + q0:QH + q0 + 128],
                              kxb[hp, KL + k0:KL + k0 + 512], False, True, tp=tp)
                    sums = ctx.pm[:, 16:17]
                    for kt in range(4):
                        mk.act(sb[kt][:, 0:512], sb[kt][:, 0:512], AF.Exp,
                               scale=0.125)
                        if kt == 0:
                            mk.red(sums, sb[kt][:, 0:512], ALU.add)
                        else:
                            mk.red(ctx.pm[:, 17:18], sb[kt][:, 0:512], ALU.add)
                            mk.tt("vector", sums, sums, ctx.pm[:, 17:18], ALU.add)
                    mk.recip(ctx.pm[:, 18:19], sums)
                    for kt in range(4):
                        ah = arcb[:, kt * 512:(kt + 1) * 512]
                        al = arcb[:, 2048 + kt * 512: 2048 + kt * 512 + 512]
                        mk.ts("vector", ah, sb[kt][:, 0:512], ctx.pm[:, 18:19],
                              None, ALU.mult)
                        mk.stt("vector", al, sb[kt][:, 0:512], ctx.pm[:, 18:19],
                               ah, ALU.mult, ALU.subtract)
                    if g == 0 and grp == 0 and h2 == 0 and sub == 0:
                        mk.tap("attnh", arcb[:, 0:2048])
                        mk.tap("attnl", arcb[:, 2048:4096])
                        mk.tap("sums", ctx.pm[:, 16:19])
                        if "exps" in DEV_TAPS:
                            eb = ctx.ar_b[:, 0:2048]
                            for kt_ in range(4):
                                mk.cp("vector", eb[:, kt_ * 512:(kt_ + 1) * 512],
                                      sb[kt_][:, 0:512])
                            mk.tap("exps", eb[:, :])
                    for kc in range(16):
                        tb = banks.take()
                        tbh = tb[:, 0:64].bitcast(BF16)
                        tbl = tb[:, 64:128].bitcast(BF16)
                        mk.pe_transpose(tbh, arcb[:, kc * 128:kc * 128 + 128],
                                        bf16=True)
                        mk.pe_transpose(tbl,
                                        arcb[:, 2048 + kc * 128:2048 + kc * 128 + 128],
                                        bf16=True)
                        ath = kxf[:, FSC:FSC + 64].bitcast(BF16)
                        atl = kxf[:, FSC + 64:FSC + 128].bitcast(BF16)
                        mk.cp("vector", ath, tbh)
                        mk.cp("vector", atl, tbl)
                        vh_t = kxb[:, VH + kc * 128 + h2 * 64: VH + kc * 128 + h2 * 64 + 64]
                        vl_t = kxb[:, VL + kc * 128 + h2 * 64: VL + kc * 128 + h2 * 64 + 64]
                        c0 = sub * 128
                        outp = avb[h2 * 64:h2 * 64 + 64, c0:c0 + 128]
                        mk.mm(outp, vh_t, ath, kc == 0, False, tp=(0, h2 * 64))
                        mk.mm(outp, vl_t, ath, False, False, tp=(0, h2 * 64))
                        mk.mm(outp, vh_t, atl, False, kc == 15, tp=(0, h2 * 64))

            banks.unpin(avi)
            t0 = grp * 256
            sh = arab[:, g * 4096 + t0: g * 4096 + t0 + 256]
            sl_ = arab[:, g * 4096 + 2048 + t0: g * 4096 + 2048 + t0 + 256]
            mk.act(sh, avb[:, 0:256], AF.Copy)
            mk.tt("vector", sl_, avb[:, 0:256], sh, ALU.subtract)

    mk.tap("slices", ctx.ar_a[:, :].bitcast(BF16))
    for oc in range(8):
        p.dma("sync", ctx.wbuf[:, 0:1024], W.proj_h[oc, :, :])
        p.dma("scalar", ctx.wbuf[:, 1024:2048], W.proj_l[oc, :, :])
        for t4 in range(4):
            bank = banks.take()
            t0, t1 = t4 * 512, t4 * 512 + 512
            first = True
            for kc in range(NCHUNK):
                wh = ctx.wbuf[:, kc * 128:(kc + 1) * 128]
                wl = ctx.wbuf[:, 1024 + kc * 128:1024 + kc * 128 + 128]
                ah = arab[:, kc * 4096 + t0: kc * 4096 + t1]
                al = arab[:, kc * 4096 + 2048 + t0: kc * 4096 + 2048 + t1]
                mk.mm(bank[:, 0:512], wh, ah, first, False)
                first = False
                mk.mm(bank[:, 0:512], wh, al, False, False)
                mk.mm(bank[:, 0:512], wl, ah, False, kc == NCHUNK - 1)
            t_ = ctx.ar_c[:, 0:512]
            mk.stt("vector", t_, ctx.xh(oc, t0, t1), sm.col("bproj", oc),
                   bank[:, 0:512], ALU.add, ALU.add)
            mk.tt("vector", t_, t_, ctx.xl(oc, t0, t1), ALU.add)
            mk.act(ctx.xh(oc, t0, t1), t_, AF.Copy)
            mk.tt("vector", ctx.xl(oc, t0, t1), t_,
                  ctx.xh(oc, t0, t1), ALU.subtract)


def emit_l1_ffn(mk, ctx, W, sm):
    banks = ctx.banks
    p = mk.p
    rowA = ctx.ar_r[0:1, 0:S]
    rowB = ctx.ar_b[0:1, 0:S]
    ln_stats(mk, ctx, rowA, None, rowB)
    rank1_bc(mk, ctx, ctx.bc0, rowA)
    acc = ctx.ar_c[:, 0:S]
    accn = ctx.kx[:, 0:2 * S].bitcast(F32)
    for fc in range(NCHUNK):
        dv = ctx.ar_a[:, fc * S:(fc + 1) * S]
        mk.tt("vector", dv, ctx.xh(fc), ctx.xl(fc), ALU.add)
        mk.tt("vector", dv, dv, ctx.bc0[:, :], ALU.subtract)
        if fc == 0:
            mk.cp("vector", acc, dv)
            mk.ts("vector", accn, dv, -1.0, None, ALU.mult)
        else:
            mk.tt("vector", acc, acc, dv, ALU.max)
            mk.stt("vector", accn, dv, -1.0, accn, ALU.mult, ALU.max)
    mk.tt("vector", acc, acc, accn, ALU.max)
    amax_pm = ctx.ar_r[:, 2032:2048]
    for tc in range(16):
        bank = banks.take()
        mk.pe_transpose(bank[:, 0:128], acc[:, tc * 128:(tc + 1) * 128])
        mk.red(amax_pm[:, tc:tc + 1], bank[:, 0:128], ALU.max)
    row_from_pm(mk, ctx, rowA, amax_pm)
    mk.tt("vector", rowA, rowA, rowB, ALU.mult)
    mk.ts("vector", rowA, rowA, 1e-5, None, ALU.max)
    rowC = ctx.ar_r[0:1, 0:S]
    rs_row = ctx.kx[:, 0:2 * S].bitcast(F32)[0:1, 0:S]
    mk.recip(rs_row, rowA)
    mk.ts("vector", rs_row, rs_row, 127.0, None, ALU.mult)
    mk.tt("vector", rs_row, rs_row, rowB, ALU.mult)
    rank1_bc(mk, ctx, ctx.bc1, rs_row)
    for fc in range(NCHUNK):
        dv = ctx.ar_a[:, fc * S:(fc + 1) * S]
        mk.tt("vector", ctx.ar_c[:, 0:S], dv, ctx.bc1[:, :], ALU.mult)
        mk.ts("vector", ctx.ar_c[:, 0:S], ctx.ar_c[:, 0:S], MAGIC, -MAGIC,
              ALU.add, ALU.add)
        mk.ts("vector", ctx.kx[:, fc * S:(fc + 1) * S], ctx.ar_c[:, 0:S],
              127.0, -128.0, ALU.min, ALU.max)
    mk.ts("vector", rowA, rowA, 1.0 / 127.0, None, ALU.mult)
    rank1_bc(mk, ctx, ctx.bc0, rowA)
    for t4 in range(4):
        t0, t1 = t4 * 512, t4 * 512 + 512
        for ic in range(32):
            wslot = ctx.wbuf[:, (ic % 2) * D:(ic % 2) * D + D]
            p.dma(("sync", "scalar")[ic % 2], wslot, W.wg[ic, :, :])
            bg = banks.take()
            for kc in range(8):
                mk.mm(bg[:, 0:512], wslot[:, kc * 128:(kc + 1) * 128],
                      ctx.kx[:, kc * S + t0:kc * S + t1], kc == 0, kc == 7)
            gs = ctx.ar_c[:, 0:512]
            mk.stt("vector", gs, bg[:, 0:512], W.qinv_col[:, 0:1], ctx.bc0[:, t0:t1],
                   ALU.mult, ALU.mult)
            mk.act(gs, gs, AF.Silu)
            wslot2 = ctx.wbuf[:, (ic % 2) * D:(ic % 2) * D + D]
            p.dma(("sync", "scalar")[ic % 2], wslot2, W.wu[ic, :, :])
            bu = banks.take()
            for kc in range(8):
                mk.mm(bu[:, 0:512], wslot2[:, kc * 128:(kc + 1) * 128],
                      ctx.kx[:, kc * S + t0:kc * S + t1], kc == 0, kc == 7)
            pr = ctx.ar_a[:, ic * 512:(ic + 1) * 512]
            mk.stt("vector", pr, bu[:, 0:512], W.qinv_col[:, 1:2], ctx.bc0[:, t0:t1],
                   ALU.mult, ALU.mult)
            mk.tt("vector", pr, pr, gs, ALU.mult)
        acc2 = ctx.ar_c[:, 0:512]
        acc2n = ctx.ar_c[:, 512:1024]
        for ic in range(32):
            pr = ctx.ar_a[:, ic * 512:(ic + 1) * 512]
            if ic == 0:
                mk.cp("vector", acc2, pr)
                mk.ts("vector", acc2n, pr, -1.0, None, ALU.mult)
            else:
                mk.tt("vector", acc2, acc2, pr, ALU.max)
                mk.stt("vector", acc2n, pr, -1.0, acc2n, ALU.mult, ALU.max)
        mk.tt("vector", acc2, acc2, acc2n, ALU.max)
        pm2 = ctx.ar_r[:, 2028:2032]
        for tc in range(4):
            bank = banks.take()
            mk.pe_transpose(bank[:, 0:128], acc2[:, tc * 128:(tc + 1) * 128])
            mk.red(pm2[:, tc:tc + 1], bank[:, 0:128], ALU.max)
        hb = mk.dram("l1f_rb", [512], F32)
        p.dma("gpsimd", hb.rearrange("(c p) -> p c", p=128), pm2, slow=True)
        am2 = ctx.ar_c[0:1, 1024:1536]
        p.dma("gpsimd", am2, hb.rearrange("(o t) -> o t", o=1))
        dq2 = ctx.ar_c[0:1, 1536:2048]
        mk.ts("vector", am2, am2, 1e-5, None, ALU.max)
        mk.ts("vector", dq2, am2, 1.0 / 127.0, None, ALU.mult)
        mk.recip(am2, am2)
        mk.ts("vector", am2, am2, 127.0, None, ALU.mult)
        for rsrc, dstbc in ((am2, ctx.bc1[:, t0:t1]), (dq2, ctx.ar_b[:, 0:512])):
            bank = banks.take()
            mk.mm(bank[:, 0:512], mk.ones_row128[:, :], rsrc, True, True)
            mk.cp("vector", dstbc, bank[:, 0:512])
        dbanks = []
        for oc in range(8):
            bi, bk = banks.take_pin()
            dbanks.append((bi, bk))
        for ic in range(32):
            pr = ctx.ar_a[:, ic * 512:(ic + 1) * 512]
            q2 = ctx.kx[:, NCHUNK * S - 512:NCHUNK * S].bitcast(BF16) if False else None
            k2 = ctx.ar_c[:, 0:512]
            mk.tt("vector", k2, pr, ctx.bc1[:, t0:t1], ALU.mult)
            mk.ts("vector", k2, k2, MAGIC, -MAGIC, ALU.add, ALU.add)
            k2b = ctx.ar_c[:, 512:768].bitcast(BF16)
            mk.ts("vector", k2b, k2, 127.0, -128.0, ALU.min, ALU.max)
            wslot = ctx.wbuf[:, (ic % 2) * D:(ic % 2) * D + D]
            p.dma(("sync", "scalar")[ic % 2], wslot, W.wd[ic, :, :])
            for oc in range(8):
                mk.mm(dbanks[oc][1][:, 0:512], wslot[:, oc * 128:(oc + 1) * 128],
                      k2b, ic == 0, ic == 31)
        for oc in range(8):
            bi, bk = dbanks[oc]
            t_ = ctx.ar_c[:, 1024:1536]
            mk.stt("vector", t_, bk[:, 0:512], W.qinv_col[:, 2:3], ctx.ar_b[:, 0:512],
                   ALU.mult, ALU.mult)
            mk.stt("vector", t_, ctx.xh(oc, t0, t1), 0.0, t_, ALU.add, ALU.add)
            mk.tt("vector", t_, t_, ctx.xl(oc, t0, t1), ALU.add)
            mk.act(ctx.xh(oc, t0, t1), t_, AF.Copy)
            mk.tt("vector", ctx.xl(oc, t0, t1), t_, ctx.xh(oc, t0, t1), ALU.subtract)
            banks.unpin(bi)


def emit_l1_ab(mk, ctx, W, sm):
    banks = ctx.banks
    mk.build_ssd_constants(banks, ctx)
    p = mk.p
    H, CH, L = 16, 32, 64
    kxf = ctx.kx.bitcast(F32)
    negmu = ctx.ar_r[0:1, 0:S]
    r_row = ctx.ar_b[0:1, 0:S]
    mu_row = ctx.ar_c[0:1, 0:S]
    ln_stats(mk, ctx, mu_row, negmu, r_row)
    rank1_bc(mk, ctx, ctx.bc0, r_row)

    def load_cs(off, n=128):
        dst = ctx.csrow[0:1, 0:n]
        p.dma("gpsimd", dst, W.inp_cs[off:off + n].rearrange("(o t) -> o t", o=1))
        return dst

    for oc in range(9):
        nr = 128 if oc < 8 else 16
        p.dma("sync", ctx.wbuf[:, 0:1024], W.inp_h[oc, :, :])
        p.dma("scalar", ctx.wbuf[:, 1024:2048], W.inp_l[oc, :, :])
        cs = load_cs(oc * 128, nr)
        xi = ctx.ar_c[:, 0:S]
        for t4 in range(4):
            bank = banks.take()
            t0, t1 = t4 * 512, t4 * 512 + 512
            first = True
            for kc in range(NCHUNK):
                whs = ctx.wbuf[:, kc * 128:kc * 128 + nr]
                wls = ctx.wbuf[:, 1024 + kc * 128:1024 + kc * 128 + nr]
                mk.mm(bank[0:nr, 0:512], whs, ctx.xh(kc, t0, t1), first, False)
                first = False
                mk.mm(bank[0:nr, 0:512], whs, ctx.xl(kc, t0, t1), False, False)
                mk.mm(bank[0:nr, 0:512], wls, ctx.xh(kc, t0, t1), False, False)
            mk.mm(bank[0:nr, 0:512], cs, negmu[0:1, t0:t1], False, True)
            mk.tt("vector", xi[0:nr, t0:t1], bank[0:nr, 0:512],
                  ctx.bc0[0:nr, t0:t1], ALU.mult)
        if oc == 0:
            mk.tap("xi0", xi)
        if oc < 8:
            xc = ctx.ar_b[:, 0:S]
            mk.ts("vector", xc, xi, sm.col("convw", oc * 4 + 3), None, ALU.mult)
            for j, sh in ((2, 1), (1, 2), (0, 3)):
                mk.stt("vector", xc[:, sh:S], xi[:, 0:S - sh],
                       sm.col("convw", oc * 4 + j), xc[:, sh:S], ALU.mult, ALU.add)
            mk.ts("vector", xc, xc, sm.col("convb", oc), None, ALU.add)
            for tc in range(16):
                bank = banks.take()
                mk.pe_transpose(bank[:, 0:128], xc[:, tc * 128:(tc + 1) * 128])
                mk.cp("vector", ctx.ar_a[:, tc * 1024 + oc * 128: tc * 1024 + oc * 128 + 128],
                      bank[:, 0:128])
        else:
            mk.tap("wrows", xi[0:16, 0:S])
            A = kxf[0:16, 0:S]
            mk.ts("vector", A, xi[0:16, 0:S], sm.col("wbase", rows=16), None, ALU.mult)
    A = kxf[0:16, 0:S]
    Acs = kxf[0:16, S:2 * S]
    pong = kxf[0:16, 2 * S:3 * S]
    runmin = kxf[0:16, 3 * S:4 * S]
    bufs = [Acs, pong]
    mk.cp("vector", Acs, A)
    cur = 0
    for sh in (1, 2, 4, 8, 16, 32):
        src_, dst_ = bufs[cur], bufs[1 - cur]
        a3 = src_.rearrange("p (c l) -> p c l", l=L)
        d3 = dst_.rearrange("p (c l) -> p c l", l=L)
        mk.cp("vector", d3[:, :, 0:sh], a3[:, :, 0:sh])
        mk.tt("vector", d3[:, :, sh:L], a3[:, :, sh:L], a3[:, :, 0:L - sh], ALU.add)
        cur = 1 - cur
    if cur != 0:
        mk.cp("vector", Acs, pong)
    bufs = [runmin, pong]
    mk.cp("vector", runmin, Acs)
    cur = 0
    for sh in (1, 2, 4, 8, 16, 32):
        src_, dst_ = bufs[cur], bufs[1 - cur]
        a3 = src_.rearrange("p (c l) -> p c l", l=L)
        d3 = dst_.rearrange("p (c l) -> p c l", l=L)
        mk.cp("vector", d3[:, :, 0:sh], a3[:, :, 0:sh])
        mk.tt("vector", d3[:, :, sh:L], a3[:, :, sh:L], a3[:, :, 0:L - sh], ALU.min)
        cur = 1 - cur
    if cur != 0:
        mk.cp("vector", runmin, pong)
    fmin = ctx.pm[0:16, 0:32]
    a3 = Acs.rearrange("p (c l) -> p c l", l=L)
    mk.red(fmin, a3, ALU.min)
    fmax_pm = ctx.bc1[0:16, 1024 + 165:1024 + 197]
    mk.red(fmax_pm, a3, ALU.max)
    ds = kxf[0:16, 2 * S:3 * S]
    sdo = ctx.ar_c[0:16, 0:S]
    fm3 = fmin.rearrange("p (c o) -> p c o", o=1)
    mk.tt("vector", ds.rearrange("p (c l) -> p c l", l=L),
          fm3.broadcast_to([16, 32, L]), a3, ALU.subtract)
    mk.act(ds, ds, AF.Exp)
    fx3 = fmax_pm.rearrange("p (c o) -> p c o", o=1)
    mk.tt("vector", sdo.rearrange("p (c l) -> p c l", l=L), a3,
          fx3.broadcast_to([16, 32, L]), ALU.subtract)
    mk.act(sdo, sdo, AF.Exp)
    pmB = ctx.bc1[0:16, 1024:1224]
    Tc = pmB[0:16, 0:32]
    mk.memset("vector", pmB[0:16, 32:33], 0.0, fence=True)
    mk.cp("vector", Tc, Acs.rearrange("p (c l) -> p c l", l=L)[:, :, L - 1:L],
          fence=True)
    Pcs = pmB[0:16, 33:66]
    mk.memset("vector", pmB[0:16, 33:34], 0.0, fence=True)
    mk.cp("vector", pmB[0:16, 34:66], Tc, fence=True)
    pong2 = pmB[0:16, 66:99]
    cur_ap, oth_ap = Pcs, pong2
    for sh in (1, 2, 4, 8, 16, 32):
        mk.cp("vector", oth_ap[:, 0:sh], cur_ap[:, 0:sh], fence=True)
        mk.tt("vector", oth_ap[:, sh:33], cur_ap[:, sh:33], cur_ap[:, 0:33 - sh],
              ALU.add, fence=True)
        cur_ap, oth_ap = oth_ap, cur_ap
    if cur_ap is not Pcs:
        mk.cp("vector", Pcs, cur_ap, fence=True)
    runP = pmB[0:16, 99:132]
    cur_ap, oth_ap = runP, pong2
    mk.cp("vector", runP, Pcs, fence=True)
    for sh in (1, 2, 4, 8, 16, 32):
        mk.cp("vector", oth_ap[:, 0:sh], cur_ap[:, 0:sh], fence=True)
        mk.tt("vector", oth_ap[:, sh:33], cur_ap[:, sh:33], cur_ap[:, 0:33 - sh],
              ALU.min, fence=True)
        cur_ap, oth_ap = oth_ap, cur_ap
    if cur_ap is not runP:
        mk.cp("vector", runP, cur_ap, fence=True)
    negPcs = pmB[0:16, 132:165]
    mk.ts("vector", negPcs, Pcs, -1.0, None, ALU.mult, fence=True)
    ds_tm = ctx.bc1[:, 0:256]
    sdo_tm = ctx.bc1[:, 256:512]
    acs_tm = ctx.bc1[:, 512:768]
    rmin_tm = ctx.bc1[:, 768:1024]
    for arr, tm in ((ds, ds_tm), (sdo, sdo_tm), (Acs, acs_tm), (runmin, rmin_tm)):
        for tc in range(16):
            bank = banks.take()
            mk.pe_transpose(bank[0:128, 0:16], arr[0:16, tc * 128:(tc + 1) * 128])
            mk.cp("vector", tm[:, tc * 16:(tc + 1) * 16], bank[0:128, 0:16])
    mk.tap("A", kxf[0:16, 0:S])
    mk.tap("Acs", kxf[0:16, S:2 * S])
    mk.tap("runmin", kxf[0:16, 3 * S:4 * S])
    mk.tap("ds", kxf[0:16, 2 * S:3 * S])
    mk.tap("sdo", ctx.ar_c[0:16, 0:S])
    mk.tap("pmB", ctx.bc1[0:16, 1024:1156])
    hbQ = mk.dram("ssdrows", [16, 66], F32)
    p.dma("gpsimd", hbQ, pmB[0:16, 99:165], slow=True)
    nsrep = ctx.ar_c[0:2, 1008:2048]
    ns_hbm = mk.dram("ns_hbm", [H, 32, 65], F32)
    ssdmask = ctx.ar_b[:, 1024:1280].bitcast(BF16)
    mk.memset("vector", ssdmask[:, :], 0.0)
    for j in range(16):
        mk.memset("vector", ssdmask[0:64, j * 32 + 2 * j: j * 32 + 2 * j + 1], 1.0)
        mk.memset("vector", ssdmask[64:128, j * 32 + 2 * j + 1: j * 32 + 2 * j + 2], 1.0)
    dstile = ctx.ar_b[:, 0:32]
    for h in range(H):
        sbank_i, sbank = banks.take_pin()
        for j in range(16):
            mk.tt("vector", dstile, ds_tm[:, j * 16 + h:j * 16 + h + 1]
                  .broadcast_to([128, 32]), ssdmask[:, j * 32:(j + 1) * 32],
                  ALU.mult, fence=True)
            xh_ = ctx.ar_a[:, j * 1024 + h * 64: j * 1024 + h * 64 + 64]
            mk.mm(sbank[0:32, 0:64], dstile, xh_, j == 0, False)
            mk.mm(sbank[0:32, 64:65], dstile, mk.ones_col_f[:, :], False, j == 15)
        states_sb = ctx.ar_b[0:32, 128:128 + 65]
        mk.cp("vector", states_sb, sbank[0:32, 0:65])
        banks.unpin(sbank_i)
        Vd1 = ctx.ar_b[:, 256:258]
        Vd2 = ctx.ar_b[:, 258:260]
        mk.memset("vector", Vd1[:, 0:2], 0.0, fence=True)
        mk.memset("vector", Vd2[:, 0:2], 0.0, fence=True)
        p.dma("gpsimd", Vd1[0:32, 0:1],
              hbQ[h, 34:66].rearrange("(p o) -> p o", o=1), slow=True)
        mk.memset("vector", Vd1[0:32, 1:2], 1.0, fence=True)
        mk.memset("vector", Vd2[0:32, 0:1], 1.0, fence=True)
        p.dma("gpsimd", Vd2[0:32, 1:2],
              hbQ[h, 0:32].rearrange("(p o) -> p o", o=1), slow=True)
        b2 = banks.take()
        mk.pe_transpose(b2[0:2, 0:128], Vd1[:, 0:2])
        lhsT2 = ctx.ar_b[0:2, 384:512]
        mk.cp("vector", lhsT2, b2[0:2, 0:128])
        b3 = banks.take()
        mk.pe_transpose(b3[0:2, 0:128], Vd2[:, 0:2])
        rhs2 = ctx.ar_b[0:2, 640:768]
        mk.cp("vector", rhs2, b3[0:2, 0:128])
        b4 = banks.take()
        mk.mm(b4[0:32, 0:32], lhsT2[0:2, 0:32], rhs2[0:2, 0:32], True, True)
        dcT = ctx.ar_b[0:32, 768:800]
        mk.tt("vector", dcT, b4[0:32, 0:32], mk.mask_zc[0:32, 0:32], ALU.add)
        mk.act(dcT, dcT, AF.Exp)
        b5 = banks.take()
        mk.mm(b5[0:32, 0:65], dcT, states_sb, True, True)
        ns_sb = ctx.ar_b[0:32, 800:865]
        mk.cp("vector", ns_sb, b5[0:32, 0:65])
        if h == 0:
            mk.tap("st0", states_sb)
            mk.tap("dcT0", dcT)
            mk.tap("ns0", ns_sb)
        p.dma("gpsimd", ns_hbm[h, :, :], ns_sb)
    mk.tap("nsall", ns_hbm[:, :, :])
    recipcol = ctx.pm[:, 20:21]
    for h in range(H):
        p.dma("gpsimd", nsrep[0:2, 0:16 * 65].rearrange("c2 (j n) -> c2 j n", n=65),
              ns_hbm[h, :, :].rearrange("(j c2) n -> c2 j n", c2=2), slow=True)
        for j in range(16):
            V1 = ctx.ar_b[:, 256:259]
            V2 = ctx.ar_b[:, 260:263]
            V3 = ctx.ar_b[:, 264:266]
            mk.ts("vector", V1[:, 0:1], acs_tm[:, j * 16 + h:j * 16 + h + 1],
                  -1.0, None, ALU.mult)
            mk.cp("vector", V1[:, 1:3], mk.pat_h2[:, :])
            mk.memset("vector", V2[:, 0:1], 1.0)
            mk.tt("vector", V2[:, 1:3], rmin_tm[:, j * 16 + h:j * 16 + h + 1]
                  .broadcast_to([128, 2]), mk.pat_h2[:, :], ALU.mult)
            mk.tt("vector", V3[:, 0:2], sdo_tm[:, j * 16 + h:j * 16 + h + 1]
                  .broadcast_to([128, 2]), mk.pat_h2[:, :], ALU.mult)
            bV = banks.take()
            mk.pe_transpose(bV[0:3, 0:128], V1[:, 0:3])
            E1 = ctx.ar_b[0:3, 384:512]
            mk.cp("vector", E1, bV[0:3, 0:128])
            bV2 = banks.take()
            mk.pe_transpose(bV2[0:3, 0:128], V2[:, 0:3])
            E2 = ctx.ar_b[0:3, 640:768]
            mk.cp("vector", E2, bV2[0:3, 0:128])
            bV3 = banks.take()
            mk.pe_transpose(bV3[0:2, 0:128], V3[:, 0:2])
            E3 = ctx.ar_b[0:2, 896:1024]
            mk.cp("vector", E3, bV3[0:2, 0:128])
            bL = banks.take()
            mk.mm(bL[:, 0:128], E1[0:3, :], E2[0:3, :], True, True)
            Lsb = ctx.ar_c[:, 0:128]
            mk.tt("vector", Lsb, bL[:, 0:128], mk.mask_ssd[:, :], ALU.add)
            mk.act(Lsb, Lsb, AF.Exp)
            bY = banks.take()
            xh_ = ctx.ar_a[:, j * 1024 + h * 64: j * 1024 + h * 64 + 64]
            mk.mm(bY[:, 0:64], Lsb, xh_, True, False)
            mk.mm(bY[:, 64:65], Lsb, mk.ones_col_f[:, :], False, False)
            mk.mm(bY[:, 0:65], E3[0:2, :], nsrep[0:2, j * 65:(j + 1) * 65],
                  False, True)
            mk.recip(recipcol, bY[:, 64:65])
            mk.ts("vector", xh_, bY[:, 0:64], recipcol, None, ALU.mult)
    yh = ctx.kx[:, 0:8192]
    yl = ctx.kx[:, 8192:16384]
    for t4 in range(4):
        t0 = t4 * 512
        for tci in range(4):
            tc = t4 * 4 + tci
            for fc in range(8):
                bank = banks.take()
                mk.pe_transpose(bank[:, 0:128],
                                ctx.ar_a[:, tc * 1024 + fc * 128: tc * 1024 + fc * 128 + 128])
                dh = yh[:, fc * 1024 + tci * 128: fc * 1024 + tci * 128 + 128]
                dl = yl[:, fc * 1024 + tci * 128: fc * 1024 + tci * 128 + 128]
                mk.act(dh, bank[:, 0:128], AF.Copy)
                mk.tt("vector", dl, bank[:, 0:128], dh, ALU.subtract)
        for oc in range(8):
            p.dma("sync", ctx.wbuf[:, 0:1024], W.outp_h[oc, :, :])
            p.dma("scalar", ctx.wbuf[:, 1024:2048], W.outp_l[oc, :, :])
            bank = banks.take()
            first = True
            for kc in range(NCHUNK):
                whs = ctx.wbuf[:, kc * 128:(kc + 1) * 128]
                wls = ctx.wbuf[:, 1024 + kc * 128:1024 + kc * 128 + 128]
                ah = yh[:, kc * 1024:kc * 1024 + 512]
                al = yl[:, kc * 1024:kc * 1024 + 512]
                mk.mm(bank[:, 0:512], whs, ah, first, False)
                first = False
                mk.mm(bank[:, 0:512], whs, al, False, False)
                mk.mm(bank[:, 0:512], wls, ah, False, kc == NCHUNK - 1)
            t_ = ctx.ar_c[:, 0:512]
            mk.tt("vector", t_, bank[:, 0:512], ctx.xh(oc, t0, t0 + 512), ALU.add)
            mk.tt("vector", t_, t_, ctx.xl(oc, t0, t0 + 512), ALU.add)
            mk.act(ctx.xh(oc, t0, t0 + 512), t_, AF.Copy)
            mk.tt("vector", ctx.xl(oc, t0, t0 + 512), t_,
                  ctx.xh(oc, t0, t0 + 512), ALU.subtract)


def exp32(mk, out, x, k, r, p):
    INV = 1.4426950408889634
    L2H, L2L = 0.693359375, -2.1219444005469057e-4
    mk.ts("vector", k, x, INV, MAGIC, ALU.mult, ALU.add)
    mk.ts("vector", k, k, -MAGIC, None, ALU.add)
    mk.stt("vector", r, k, -L2H, x, ALU.mult, ALU.add)
    mk.stt("vector", r, k, -L2L, r, ALU.mult, ALU.add)
    C = [1.0 / 362880, 1.0 / 40320, 1.0 / 5040, 1.0 / 720, 1.0 / 120,
         1.0 / 24, 1.0 / 6, 0.5, 1.0, 1.0]
    mk.memset("vector", p, C[0])
    for c in C[1:]:
        mk.tt("vector", p, p, r, ALU.mult)
        mk.ts("vector", p, p, c, None, ALU.add)
    ki = k.bitcast(I32)
    mk.cp("vector", ki, k)
    mk.ts("vector", ki, ki, 127, None, ALU.add)
    mk.ts("vector", ki, ki, 23, None, ALU.arith_shift_left)
    mk.tt("vector", out, p, k, ALU.mult)


def emit_l0_attn_b(mk, ctx, W, sm, ins):
    p = mk.p
    banks = ctx.banks
    kxf = ctx.kx.bitcast(F32)
    arab = ctx.ar_a.bitcast(BF16)
    arbi = ctx.ar_b.bitcast(I32)
    NT = 16

    negmu = ctx.ar_r[0:1, 0:S]
    r_row = ctx.ar_b[0:1, 0:S]
    mu_row = ctx.ar_c[0:1, 0:S]
    ln_stats(mk, ctx, mu_row, negmu, r_row)
    rank1_bc(mk, ctx, ctx.bc0, r_row)
    mrow = ctx.ar_c[0:1, 0:S]
    mk.tt("vector", mrow, mu_row, ctx.bc0[0:1, 0:S], ALU.mult)
    rank1_bc(mk, ctx, ctx.bc1, mrow)
    if not hasattr(W, "ab_ln_hbm"):
        W.ab_ln_hbm = mk.dram("ab_ln", [NT, 128, D], F32)
        W.ab_km_hbm = mk.dram("ab_km", [NT, 4, 128, 128], F32)
    for fc in range(NCHUNK):
        t = ctx.ar_c[:, 0:S]
        mk.tt("vector", t, ctx.xh(fc), ctx.xl(fc), ALU.add)
        mk.tt("vector", t, t, ctx.bc0[:, :], ALU.mult)
        mk.tt("vector", t, t, ctx.bc1[:, :], ALU.subtract)
        for tc in range(NT):
            bank = banks.take()
            mk.pe_transpose(bank[:, 0:128], t[:, tc * 128:(tc + 1) * 128])
            st = ctx.ar_b[:, 256 + (tc % 2) * 128:256 + (tc % 2) * 128 + 128]
            mk.cp("vector", st, bank[:, 0:128])
            p.dma(("sync", "scalar")[tc % 2], W.ab_ln_hbm[tc, :, fc * 128:(fc + 1) * 128], st)
    p.dma("sync", ctx.wbuf[:, 0:1024], W.wv_h[0, :, :])
    p.dma("scalar", ctx.wbuf[:, 1024:2048], W.wv_l[0, :, :])
    csw = ctx.csrow[0:1, 0:16]
    p.dma("gpsimd", csw, W.wv_cs[0:16].rearrange("(o t) -> o t", o=1))
    wave = ctx.ar_b[0:16, 0:S]
    for t4 in range(4):
        t0, t1 = t4 * 512, t4 * 512 + 512
        bank = banks.take()
        first = True
        for kc in range(NCHUNK):
            wh = ctx.wbuf[:, kc * 128:kc * 128 + 16]
            wl = ctx.wbuf[:, 1024 + kc * 128:1024 + kc * 128 + 16]
            mk.mm(bank[0:16, 0:512], wh, ctx.xh(kc, t0, t1), first, False)
            first = False
            mk.mm(bank[0:16, 0:512], wh, ctx.xl(kc, t0, t1), False, False)
            mk.mm(bank[0:16, 0:512], wl, ctx.xh(kc, t0, t1), False, False)
        mk.mm(bank[0:16, 0:512], csw, negmu[0:1, t0:t1], False, True)
        mk.tt("vector", wave[:, t0:t1], bank[0:16, 0:512], ctx.bc0[0:16, t0:t1],
              ALU.mult)
        mk.ts("vector", wave[:, t0:t1], wave[:, t0:t1],
              sm.col("wvb", rows=16), None, ALU.add)
    freqf = ctx.ar_b[0:8, 0:S]
    phasef = ctx.ar_c[0:8, 0:S]
    we = kxf[0:8, 0:S]
    wo = kxf[0:8, S:2 * S]
    kbuf = kxf[0:8, 2 * S:3 * S]
    rbuf = kxf[0:8, 3 * S:4 * S]
    pbuf = phasef
    for t4 in range(4):
        t0, t1 = t4 * 512, t4 * 512 + 512
        be = banks.take()
        mk.mm(be[0:8, 0:512], mk.sel_e[:, :], wave[:, t0:t1], True, True)
        bo = banks.take()
        mk.mm(bo[0:8, 0:512], mk.sel_o[:, :], wave[:, t0:t1], True, True)
        mk.cp("vector", we[:, t0:t1], be[0:8, 0:512])
        mk.cp("vector", wo[:, t0:t1], bo[0:8, 0:512])
    mk.ts("vector", we, we, 30.0, -30.0, ALU.min, ALU.max)
    mk.ts("vector", we, we, -1.0, None, ALU.mult)
    exp32(mk, we, we, kbuf, rbuf, pbuf)
    mk.ts("vector", we, we, 1.0, None, ALU.add)
    mk.recip(we, we)
    mk.ts("vector", freqf, we, 15.0, 1.0, ALU.mult, ALU.add)
    mk.ts("vector", wo, wo, 10.0, -10.0, ALU.min, ALU.max)
    mk.ts("vector", wo, wo, 2.0, None, ALU.mult)
    exp32(mk, wo, wo, kbuf, rbuf, pbuf)
    mk.ts("vector", kbuf, wo, 1.0, None, ALU.subtract)
    mk.ts("vector", rbuf, wo, 1.0, None, ALU.add)
    mk.recip(rbuf, rbuf)
    mk.tt("vector", phasef, kbuf, rbuf, ALU.mult)
    mk.tt("vector", phasef, phasef, freqf, ALU.mult)
    for tc in range(NT):
        bank = banks.take()
        mk.pe_transpose(bank[0:128, 0:8], freqf[:, tc * 128:(tc + 1) * 128])
        mk.cp("vector", ctx.bc1[:, tc * 8:tc * 8 + 8], bank[0:128, 0:8],
              fence=True)
        bank = banks.take()
        mk.pe_transpose(bank[0:128, 0:8], phasef[:, tc * 128:(tc + 1) * 128])
        mk.cp("vector", ctx.bc1[:, 128 + tc * 8:128 + tc * 8 + 8],
              bank[0:128, 0:8], fence=True)
    for oc in range(4):
        p.dma("sync", ctx.wbuf[:, 0:1024], W.kn_h[oc, :, :])
        p.dma("scalar", ctx.wbuf[:, 1024:2048], W.kn_l[oc, :, :])
        csk = ctx.csrow[0:1, 0:128]
        p.dma("gpsimd", csk, W.kn_cs[oc * 128:(oc + 1) * 128]
              .rearrange("(o t) -> o t", o=1))
        km = ctx.ar_c[:, 0:S]
        for t4 in range(4):
            t0, t1 = t4 * 512, t4 * 512 + 512
            bank = banks.take()
            first = True
            for kc in range(NCHUNK):
                wh = ctx.wbuf[:, kc * 128:(kc + 1) * 128]
                wl = ctx.wbuf[:, 1024 + kc * 128:1024 + kc * 128 + 128]
                mk.mm(bank[:, 0:512], wh, ctx.xh(kc, t0, t1), first, False)
                first = False
                mk.mm(bank[:, 0:512], wh, ctx.xl(kc, t0, t1), False, False)
                mk.mm(bank[:, 0:512], wl, ctx.xh(kc, t0, t1), False, False)
            mk.mm(bank[:, 0:512], csk, negmu[0:1, t0:t1], False, True)
            mk.tt("vector", km[:, t0:t1], bank[:, 0:512], ctx.bc0[:, t0:t1],
                  ALU.mult)
            mk.ts("vector", km[:, t0:t1], km[:, t0:t1], sm.col("knb", oc),
                  None, ALU.add)
        for tc in range(NT):
            bank = banks.take()
            mk.pe_transpose(bank[:, 0:128], km[:, tc * 128:(tc + 1) * 128])
            st = ctx.ar_b[:, 256 + (tc % 2) * 128:256 + (tc % 2) * 128 + 128]
            mk.cp("vector", st, bank[:, 0:128])
            p.dma(("sync", "scalar")[tc % 2], W.ab_km_hbm[tc, oc, :, :], st)
    bc0i = ctx.bc0.bitcast(I32)
    p.emit("gpsimd", lambda e: e.iota(bc0i[:, 0:S], [[1, S]], base=0,
                                      channel_multiplier=0), writes=[bc0i[:, 0:S]])
    mk.ts("vector", ctx.bc0[:, 0:S], bc0i[:, 0:S], 0, None, ALU.add, fence=True)
    iotaF = ctx.bc0[:, 0:S]
    tsrow_i = arbi[:, 1952:1984]
    p.emit("gpsimd", lambda e: e.iota(tsrow_i[:, :], [[1, 32]], base=0,
                                      channel_multiplier=0), writes=[tsrow_i[:, :]])
    tsrow = ctx.ar_b[:, 1952:1984]
    mk.ts("vector", tsrow, tsrow_i, 0, None, ALU.add, fence=True)
    mk.ts("vector", tsrow, tsrow, -15.5, None, ALU.add, fence=True)
    pcol_i = arbi[:, 1984:1985]
    p.emit("gpsimd", lambda e: e.iota(pcol_i[:, :], [[0, 1]], base=0,
                                      channel_multiplier=1), writes=[pcol_i[:, :]])
    pcolf = ctx.ar_b[:, 1984:1985]
    mk.ts("vector", pcolf, pcol_i, 0, None, ALU.add, fence=True)
    iota_b = arbi[:, 1856:1920]
    p.emit("gpsimd", lambda e: e.iota(iota_b[:, :], [[1, 64]], base=0,
                                      channel_multiplier=0), writes=[iota_b[:, :]])
    mk.tap("abfq", ctx.bc1[:, 0:256])
    o8 = ctx.ar_b[:, 0:256]
    pos8 = ctx.ar_b[:, 256:512]
    negp8 = ctx.ar_b[:, 512:768]
    bins8f = ctx.ar_b[:, 768:1024]
    w8 = ctx.ar_b[:, 1024:1280]
    negw8 = ctx.ar_b[:, 1280:1536]
    bins8i = arbi[:, 1536:1792]
    ktile = ctx.ar_b[:, 1792:1856]
    loaded = set()

    def ensure_chunk(c):
        if c in loaded or not (0 <= c < NT):
            return
        loaded.add(c)
        p.dma(("sync", "scalar")[c % 2], kxf[:, (c % 8) * 1024:(c % 8) * 1024 + 1024],
              W.ab_ln_hbm[c, :, :])

    def tap_bounds(s, t0):
        tsv = s - 15.5
        if tsv >= 0:
            lo, hi = -16.0 + tsv, 16.0 + 16.0 * tsv
        else:
            lo, hi = -16.0 + 16.0 * tsv, 16.0 + tsv
        c0 = int(np.floor(t0 + lo)) - 1
        c1 = int(np.ceil(t0 + 127 + hi)) + 2
        return c0, c1

    for tc in range(NT):
        t0 = tc * 128
        lo_all = min(tap_bounds(s, t0)[0] for s in range(32))
        hi_all = max(tap_bounds(s, t0)[1] for s in range(32))
        c_lo = max(0, lo_all // 128)
        c_hi = min(NT, (hi_all + 127) // 128)
        bchunks = list(range(c_lo, c_hi))
        base = c_lo * 128
        Wt = len(bchunks) * 128
        for c in bchunks:
            ensure_chunk(c)
        fcols = ctx.bc1[:, tc * 8:tc * 8 + 8]
        pcols = ctx.bc1[:, 128 + tc * 8:128 + tc * 8 + 8]
        o3 = o8.rearrange("p (h s) -> p h s", s=32)
        mk.tt("vector", o3, tsrow.rearrange("p (o s) -> p o s", o=1)
              .broadcast_to([128, 8, 32]),
              fcols.rearrange("p (h o) -> p h o", o=1).broadcast_to([128, 8, 32]),
              ALU.mult, fence=True)
        mk.tt("vector", o3, o3, pcols.rearrange("p (h o) -> p h o", o=1)
              .broadcast_to([128, 8, 32]), ALU.add, fence=True)
        mk.ts("vector", pos8, o8, pcolf, float(t0), ALU.add, ALU.add, fence=True)
        mk.ts("vector", pos8, pos8, 0.0, 2047.0, ALU.max, ALU.min, fence=True)
        mk.ts("vector", negp8, pos8, -1.0, None, ALU.mult, fence=True)
        mk.ts("vector", bins8f, o8, 256.0, 0.125, ALU.add, ALU.mult, fence=True)
        mk.ts("vector", bins8f, bins8f, 0.0, 63.0, ALU.max, ALU.min, fence=True)
        mk.cp("vector", bins8i, bins8f, fence=True)
        mk.cp("vector", w8, bins8i, fence=True)
        mk.tt("vector", bins8f, bins8f, w8, ALU.subtract, fence=True)
        mk.ts("vector", bins8f, bins8f, 0.0, -1.0, ALU.is_lt, ALU.mult, fence=True)
        mk.tt("vector", bins8f, bins8f, w8, ALU.add, fence=True)
        mk.cp("vector", bins8i, bins8f, fence=True)
        if tc == 5:
            mk.tap("abpos", pos8)
            mk.tap("abbins", bins8f)
            mk.tap("abbini", bins8i)
        for h in range(8):
            p.dma("gpsimd", ktile, W.ab_km_hbm[tc, h // 2, :, (h % 2) * 64:
                                               (h % 2) * 64 + 64])
            arri = ctx.ar_r.bitcast(I32)
            s3i = arri[:, 0:2048].rearrange("p (s b) -> p s b", b=64)
            mk.cp("vector", s3i,
                  bins8i[:, h * 32:(h + 1) * 32].rearrange("p (s o) -> p s o", o=1)
                  .broadcast_to([128, 32, 64]))
            s3 = ctx.ar_r[:, 0:2048].rearrange("p (s b) -> p s b", b=64)
            mk.tt("vector", s3, iota_b.rearrange("p (o b) -> p o b", o=1)
                  .broadcast_to([128, 32, 64]), s3i, ALU.is_equal)
            mk.tt("vector", s3, s3, ktile.rearrange("p (o b) -> p o b", o=1)
                  .broadcast_to([128, 32, 64]), ALU.mult)
            if tc == 5 and h == 0 and "abselsum" in DEV_TAPS:
                mk.red(ctx.ar_b[:, 1920:1952], s3, ALU.add)
                mk.tap("abselsum", ctx.ar_b[:, 1920:1952])
            mk.red(w8[:, h * 32:(h + 1) * 32], s3, ALU.add)
            mk.ts("vector", negw8[:, h * 32:(h + 1) * 32],
                  w8[:, h * 32:(h + 1) * 32], -1.0, None, ALU.mult, fence=True)
            C = ctx.ar_r[:, 0:Wt]
            mk.memset("vector", C, 0.0)
            for s in range(32):
                c0, c1 = tap_bounds(s, t0)
                c0 = max(c0, base)
                c1 = min(c1, base + Wt, 2048)
                if c1 <= c0:
                    continue
                Wc = c1 - c0
                m_ = ctx.ar_r[:, 896:896 + Wc]
                t3 = ctx.ar_r[:, 1344:1344 + Wc]
                mk.act(m_, iotaF[:, c0:c1], AF.Abs,
                       bias=negp8[:, h * 32 + s:h * 32 + s + 1])
                mk.ts("vector", t3, m_, 1.0, 0.0, ALU.subtract, ALU.min)
                mk.stt("vector", C[:, c0 - base:c1 - base], t3,
                       negw8[:, h * 32 + s:h * 32 + s + 1],
                       C[:, c0 - base:c1 - base], ALU.mult, ALU.add)
            if tc == 5 and h == 0:
                mk.tap("abw", w8[:, 0:32])
                mk.tap("abC", C)
                mk.tap("abkt", ktile)
            yi, ybank = banks.take_pin()
            for j, c in enumerate(bchunks):
                tb = banks.take()
                mk.pe_transpose(tb[:, 0:128], C[:, j * 128:(j + 1) * 128])
                ct = ctx.ar_c[:, 1024 + j * 128:1024 + j * 128 + 128]
                mk.cp("vector", ct, tb[:, 0:128])
                mk.mm(ybank[:, 0:128], ct,
                      kxf[:, (c % 8) * 1024 + h * 128:(c % 8) * 1024 + h * 128 + 128],
                      j == 0, j == len(bchunks) - 1)
            banks.unpin(yi)
            ysb = ctx.ar_c[:, 0:128]
            mk.cp("vector", ysb, ybank[:, 0:128])
            tb2 = banks.take()
            mk.pe_transpose(tb2[:, 0:128], ysb)
            sh = arab[:, h * 4096 + t0: h * 4096 + t0 + 128]
            sl_ = arab[:, h * 4096 + 2048 + t0: h * 4096 + 2048 + t0 + 128]
            mk.act(sh, tb2[:, 0:128], AF.Copy)
            mk.tt("vector", sl_, tb2[:, 0:128], sh, ALU.subtract)
    mk.tap("aby", ctx.ar_a[:, :].bitcast(BF16))
    for oc in range(8):
        p.dma("sync", ctx.wbuf[:, 0:1024], W.ow_h[oc, :, :])
        p.dma("scalar", ctx.wbuf[:, 1024:2048], W.ow_l[oc, :, :])
        for t4 in range(4):
            bank = banks.take()
            t0, t1 = t4 * 512, t4 * 512 + 512
            first = True
            for kc in range(NCHUNK):
                wh = ctx.wbuf[:, kc * 128:(kc + 1) * 128]
                wl = ctx.wbuf[:, 1024 + kc * 128:1024 + kc * 128 + 128]
                ah = arab[:, kc * 4096 + t0: kc * 4096 + t1]
                al = arab[:, kc * 4096 + 2048 + t0: kc * 4096 + 2048 + t1]
                mk.mm(bank[:, 0:512], wh, ah, first, False)
                first = False
                mk.mm(bank[:, 0:512], wh, al, False, False)
                mk.mm(bank[:, 0:512], wl, ah, False, kc == NCHUNK - 1)
            t_ = ctx.ar_c[:, 0:512]
            mk.tt("vector", t_, bank[:, 0:512], ctx.xh(oc, t0, t1), ALU.add)
            mk.tt("vector", t_, t_, ctx.xl(oc, t0, t1), ALU.add)
            mk.act(ctx.xh(oc, t0, t1), t_, AF.Copy)
            mk.tt("vector", ctx.xl(oc, t0, t1), t_,
                  ctx.xh(oc, t0, t1), ALU.subtract)


def emit_l1_attn_a(mk, ctx, W, sm, ins):
    p = mk.p
    banks = ctx.banks
    kxb = ctx.kx
    kxf = ctx.kx.bitcast(F32)
    arab = ctx.ar_a.bitcast(BF16)
    arcb = ctx.ar_c.bitcast(BF16)
    QH, QL, KH, KL, VH, VL, FSC = 0, 2048, 4096, 6144, 8192, 10240, 6144

    negmu = ctx.ar_r[0:1, 0:S]
    r_row = ctx.ar_b[0:1, 0:S]
    mu_row = ctx.ar_c[0:1, 0:S]
    ln_stats(mk, ctx, mu_row, negmu, r_row)
    rank1_bc(mk, ctx, ctx.bc0, r_row)
    mrow = ctx.bc1[0:1, 0:S]
    mk.tt("vector", mrow, negmu, ctx.bc0[0:1, 0:S], ALU.mult)
    mred = ctx.ar_b[0:1, 256:320]
    mk.red(mred, mrow.rearrange("p (l c) -> p l c", c=32), ALU.add)
    mk.ts("vector", mred, mred, 1.0 / 32, None, ALU.mult, fence=True)
    mk.build_l1aa_constants(ctx.banks, ctx, ins)

    def load_cs(cs_dram, off, n=128):
        dst = ctx.csrow[0:1, 0:n]
        p.dma("gpsimd", dst, cs_dram[off:off + n].rearrange("(o t) -> o t", o=1))
        return dst

    def fold_fm(slab_h, slab_l, cs_dram, oc, dst_cb):
        p.dma("sync", ctx.wbuf[:, 0:1024], slab_h[oc, :, :])
        p.dma("scalar", ctx.wbuf[:, 1024:2048], slab_l[oc, :, :])
        cs = load_cs(cs_dram, oc * 128)
        for t4 in range(4):
            bank = banks.take()
            t0, t1 = t4 * 512, t4 * 512 + 512
            first = True
            for kc in range(NCHUNK):
                wh_t = ctx.wbuf[:, kc * 128:(kc + 1) * 128]
                wl = ctx.wbuf[:, 1024 + kc * 128:1024 + kc * 128 + 128]
                mk.mm(bank[:, 0:512], wh_t, ctx.xh(kc, t0, t1), first, False)
                first = False
                mk.mm(bank[:, 0:512], wh_t, ctx.xl(kc, t0, t1), False, False)
                mk.mm(bank[:, 0:512], wl, ctx.xh(kc, t0, t1), False, False)
            mk.mm(bank[:, 0:512], cs, negmu[0:1, t0:t1], False, True)
            dst_cb(t4, bank)

    for g in range(8):
        for which, oc, dsth, dstl in (
                ("q", g, kxb[:, QH:QH + S], kxb[:, QL:QL + S]),
                ("k", 8 + g, kxb[:, KH:KH + S], kxb[:, KL:KL + S])):
            bias_col = sm.col("l1qkv_b", oc)
            def cb(t4, bank, dsth=dsth, dstl=dstl, bias_col=bias_col):
                t0, t1 = t4 * 512, t4 * 512 + 512
                tmp = ctx.ar_c[:, 0:512]
                mk.tt("vector", tmp, bank[:, 0:512], ctx.bc0[:, t0:t1], ALU.mult)
                mk.ts("vector", tmp, tmp, bias_col, None, ALU.add)
                mk.act(dsth[:, t0:t1], tmp, AF.Copy)
                mk.tt("vector", dstl[:, t0:t1], tmp, dsth[:, t0:t1], ALU.subtract)
            fold_fm(W.aqkv_h, W.aqkv_l, W.aqkv_cs, oc, cb)
        vfm = ctx.ar_a[:, g * 2048:(g + 1) * 2048]
        vrow_sb = ctx.ar_c[0:1, 1536:1664]
        p.dma("gpsimd", vrow_sb,
              ins["L1_attn_a_qkv_b"][2048 + g * 128: 2048 + g * 128 + 128]
              .rearrange("(o t) -> o t", o=1))
        vb_bank = banks.take()
        mk.mm(vb_bank[:, 0:128], mk.ones_row128[:, :], vrow_sb, True, True)
        vbias = ctx.ar_c[:, 1024:1152]
        mk.cp("vector", vbias, vb_bank[:, 0:128])

        def cb_v(t4, bank, vfm=vfm):
            t0, t1 = t4 * 512, t4 * 512 + 512
            mk.tt("vector", vfm[:, t0:t1], bank[:, 0:512], ctx.bc0[:, t0:t1],
                  ALU.mult)
        fold_fm(W.aqkv_h, W.aqkv_l, W.aqkv_cs, 16 + g, cb_v)
        for tc in range(16):
            bank = banks.take()
            mk.pe_transpose(bank[:, 0:128], vfm[:, tc * 128:(tc + 1) * 128])
            vt = ctx.ar_c[:, 1152:1280]
            mk.tt("vector", vt, bank[:, 0:128], vbias, ALU.add)
            mk.act(kxb[:, VH + tc * 128:VH + tc * 128 + 128], vt, AF.Copy)
            mk.tt("vector", kxb[:, VL + tc * 128:VL + tc * 128 + 128], vt,
                  kxb[:, VH + tc * 128:VH + tc * 128 + 128], ALU.subtract)
        for sub in range(16):
            q0 = sub * 128
            wdt = 128 if sub == 0 else 256
            k0 = 0 if sub == 0 else q0 - 128
            mask = mk.mband[:, 128:256] if sub == 0 else mk.mband[:, 0:256]
            avi, avb = banks.take_pin()
            for h2 in range(2):
                hp = slice(h2 * 64, h2 * 64 + 64)
                tp = (h2 * 64, 0)
                sbank = banks.take()
                mk.mm(sbank[:, 0:wdt], kxb[hp, QH + q0:QH + q0 + 128],
                      kxb[hp, KH + k0:KH + k0 + wdt], True, False, tp=tp)
                mk.mm(sbank[:, 0:wdt], kxb[hp, QL + q0:QL + q0 + 128],
                      kxb[hp, KH + k0:KH + k0 + wdt], False, False, tp=tp)
                mk.mm(sbank[:, 0:wdt], kxb[hp, QH + q0:QH + q0 + 128],
                      kxb[hp, KL + k0:KL + k0 + wdt], False, True, tp=tp)
                masked = ctx.ar_c[:, 0:wdt]
                mk.tt("vector", masked, sbank[:, 0:wdt], mask, ALU.add)
                mk.red(ctx.pm[:, 20:21], masked, ALU.max)
                mk.ts("vector", ctx.pm[:, 21:22], ctx.pm[:, 20:21], -0.125, None,
                      ALU.mult, fence=True)
                probs = ctx.ar_c[:, 256:256 + wdt]
                mk.act(probs, masked, AF.Exp, bias=ctx.pm[:, 21:22], scale=0.125)
                mk.red(ctx.pm[:, 22:23], probs, ALU.add)
                mk.recip(ctx.pm[:, 23:24], ctx.pm[:, 22:23])
                pn = ctx.ar_c[:, 512:512 + wdt]
                mk.ts("vector", pn, probs, ctx.pm[:, 23:24], None, ALU.mult)
                if g == 0 and sub == 1 and h2 == 0:
                    if "dscore" in DEV_TAPS:
                        sc_sb = ctx.ar_b[:, 512:768]
                        mk.cp("vector", sc_sb, sbank[:, 0:wdt])
                        mk.tap("dscore", sc_sb)
                    mk.tap("dmasked", masked)
                    mk.tap("dprobs", probs)
                    mk.tap("dpn", pn)
                    mk.tap("dpm", ctx.pm[:, 20:24])
                first = True
                for kcn in range(wdt // 128):
                    tb = banks.take()
                    mk.pe_transpose(tb[:, 0:128], pn[:, kcn * 128:kcn * 128 + 128])
                    ath = kxf[:, FSC:FSC + 64].bitcast(BF16)
                    atl = kxf[:, FSC + 64:FSC + 128].bitcast(BF16)
                    mk.act(ath, tb[:, 0:128], AF.Copy)
                    mk.tt("vector", atl, tb[:, 0:128], ath, ALU.subtract)
                    ck = (k0 + kcn * 128) // 128
                    vh_t = kxb[:, VH + ck * 128 + h2 * 64: VH + ck * 128 + h2 * 64 + 64]
                    vl_t = kxb[:, VL + ck * 128 + h2 * 64: VL + ck * 128 + h2 * 64 + 64]
                    outp = avb[h2 * 64:h2 * 64 + 64, 0:128]
                    last = kcn == wdt // 128 - 1
                    mk.mm(outp, vh_t, ath, first, False, tp=(0, h2 * 64))
                    mk.mm(outp, vl_t, ath, False, False, tp=(0, h2 * 64))
                    mk.mm(outp, vh_t, atl, False, last, tp=(0, h2 * 64))
                    first = False
            banks.unpin(avi)
            if g == 0 and sub == 1 and "dav" in DEV_TAPS:
                av_sb = ctx.ar_b[:, 768:896]
                mk.cp("vector", av_sb, avb[:, 0:128])
                mk.tap("dav", av_sb)
            sh = arab[:, g * 4096 + q0: g * 4096 + q0 + 128]
            sl_ = arab[:, g * 4096 + 2048 + q0: g * 4096 + 2048 + q0 + 128]
            mk.act(sh, avb[:, 0:128], AF.Copy)
            mk.tt("vector", sl_, avb[:, 0:128], sh, ALU.subtract)
            if sub == 0:
                mk.memset("vector", arab[:, g * 4096:g * 4096 + 1], 0.0, fence=True)
                mk.memset("vector", arab[:, g * 4096 + 2048:g * 4096 + 2049], 0.0,
                          fence=True)
        if g == 0:
            mk.tap("aaq", kxb[:, QH:QH + S])
            mk.tap("aaql", kxb[:, QL:QL + S])
            mk.tap("aak", kxb[:, KH:KH + S])
            mk.tap("aakl", kxb[:, KL:KL + S])
            mk.tap("aav", kxb[:, VH:VH + S])
            mk.tap("aavl", kxb[:, VL:VL + S])
    mk.tap("aaloc", ctx.ar_a[:, :].bitcast(BF16))

    for lp in range(2):
        bias_col = sm.col("l1rq_b", lp)
        def cb_rq(t4, bank, lp=lp, bias_col=bias_col):
            t0, t1 = t4 * 512, t4 * 512 + 512
            tmp = ctx.ar_c[:, 0:512]
            mk.tt("vector", tmp, bank[:, 0:512], ctx.bc0[:, t0:t1], ALU.mult)
            mk.ts("vector", tmp, tmp, bias_col, None, ALU.add)
            mk.act(kxb[:, lp * 2048 + t0:lp * 2048 + t1], tmp, AF.Copy)
            mk.tt("vector", kxb[:, 4096 + lp * 2048 + t0:4096 + lp * 2048 + t1],
                  tmp, kxb[:, lp * 2048 + t0:lp * 2048 + t1], ALU.subtract)
        fold_fm(W.arq_h, W.arq_l, W.arq_cs, lp, cb_rq)
    mb_bank = banks.take()
    mk.mm(mb_bank[:, 0:64], mk.ones_row128[:, :], mred, True, True)
    m_bc = ctx.bc1[:, 1088:1152]
    mk.cp("vector", m_bc, mb_bank[:, 0:64], fence=True)
    pooled = ctx.ar_b[:, 512:1024]
    for fc in range(NCHUNK):
        tmp = ctx.ar_c[:, 0:S]
        mk.tt("vector", tmp, ctx.xh(fc), ctx.xl(fc), ALU.add)
        mk.tt("vector", tmp, tmp, ctx.bc0[:, :], ALU.mult)
        dst = pooled[:, fc * 64:(fc + 1) * 64]
        mk.red(dst, tmp.rearrange("p (l c) -> p l c", c=32), ALU.add)
        mk.ts("vector", dst, dst, 1.0 / 32, None, ALU.mult, fence=True)
        mk.tt("vector", dst, dst, m_bc, ALU.add, fence=True)
    mk.tap("aapool", pooled)
    rkfm = ctx.ar_b[:, 1024:1152]
    rvfm = ctx.ar_b[:, 1280:1408]
    for oc in range(4):
        bi, bank = banks.take_pin()
        for kc in range(NCHUNK):
            wslot = ctx.ar_c[:, (kc % 2) * 128:(kc % 2) * 128 + 128]
            p.dma("sync", wslot,
                  ins["L1_attn_a_rkv_w"][kc * 128:(kc + 1) * 128,
                                         oc * 128:(oc + 1) * 128])
            mk.mm(bank[:, 0:64], wslot, pooled[:, kc * 64:(kc + 1) * 64],
                  kc == 0, kc == NCHUNK - 1)
        dst = (rkfm if oc < 2 else rvfm)[:, (oc % 2) * 64:(oc % 2) * 64 + 64]
        mk.ts("vector", dst, bank[:, 0:64], sm.col("l1rkv_b", oc), None, ALU.add)
        banks.unpin(bi)
    rkh = ctx.wbuf[:, 0:128]
    rkl = ctx.wbuf[:, 128:256]
    mk.act(rkh, rkfm, AF.Copy)
    mk.tt("vector", rkl, rkfm, rkh, ALU.subtract, fence=True)
    for blk in range(2):
        tb = banks.take()
        mk.pe_transpose(tb[0:64, 0:128], rvfm[:, blk * 64:(blk + 1) * 64])
        mk.cp("vector", ctx.ar_b[0:64, 1536 + blk * 128:1536 + blk * 128 + 128],
              tb[0:64, 0:128])
    rvT = [ctx.ar_b[0:64, 1536 + (lh // 2) * 128 + (lh % 2) * 64:
                    1536 + (lh // 2) * 128 + (lh % 2) * 64 + 64]
           for lh in range(4)]
    for lp in range(2):
        for sub in range(16):
            q0 = sub * 128
            avi, avb = banks.take_pin()
            maskt = ctx.ar_c[:, 128:192]
            mk.ts("vector", maskt, mk.latbase[:, :], float(4 * sub), -1e9,
                  ALU.is_ge, ALU.mult, fence=True)
            for lh2 in range(2):
                hp = slice(lh2 * 64, lh2 * 64 + 64)
                sbank = banks.take()
                mk.mm(sbank[:, 0:64], kxb[hp, lp * 2048 + q0:lp * 2048 + q0 + 128],
                      rkh[hp, lp * 64:lp * 64 + 64], True, False, tp=(lh2 * 64, 0))
                mk.mm(sbank[:, 0:64], kxb[hp, 4096 + lp * 2048 + q0:
                                          4096 + lp * 2048 + q0 + 128],
                      rkh[hp, lp * 64:lp * 64 + 64], False, False, tp=(lh2 * 64, 0))
                mk.mm(sbank[:, 0:64], kxb[hp, lp * 2048 + q0:lp * 2048 + q0 + 128],
                      rkl[hp, lp * 64:lp * 64 + 64], False, True, tp=(lh2 * 64, 0))
                masked = ctx.ar_c[:, 0:64]
                mk.tt("vector", masked, sbank[:, 0:64], maskt, ALU.add)
                mk.red(ctx.pm[:, 20:21], masked, ALU.max)
                mk.ts("vector", ctx.pm[:, 21:22], ctx.pm[:, 20:21], -0.125, None,
                      ALU.mult, fence=True)
                probs = ctx.ar_c[:, 64:128]
                mk.act(probs, masked, AF.Exp, bias=ctx.pm[:, 21:22], scale=0.125)
                mk.red(ctx.pm[:, 22:23], probs, ALU.add)
                mk.recip(ctx.pm[:, 23:24], ctx.pm[:, 22:23])
                pn = ctx.ar_c[:, 192:256]
                mk.ts("vector", pn, probs, ctx.pm[:, 23:24], None, ALU.mult)
                tb = banks.take()
                mk.pe_transpose(tb[0:64, 0:128], pn)
                attnT = ctx.ar_c[0:64, 384:512]
                mk.cp("vector", attnT, tb[0:64, 0:128])
                lh = 2 * lp + lh2
                mk.mm(avb[lh2 * 64:lh2 * 64 + 64, 0:128], rvT[lh], attnT,
                      True, True, tp=(0, lh2 * 64))
            banks.unpin(avi)
            lath = kxb[:, 8192 + lp * 2048 + q0: 8192 + lp * 2048 + q0 + 128]
            latl = kxb[:, 12288 + lp * 2048 + q0: 12288 + lp * 2048 + q0 + 128]
            mk.act(lath, avb[:, 0:128], AF.Copy)
            mk.tt("vector", latl, avb[:, 0:128], lath, ALU.subtract)
            if sub == 0:
                mk.memset("vector", lath[:, 0:32], 0.0, fence=True)
                mk.memset("vector", latl[:, 0:32], 0.0, fence=True)

    mk.tap("aark", ctx.ar_b[:, 1024:1408])
    mk.tap("aalat", kxb[:, 8192:16384])
    for g in range(8):
        p.dma("sync", ctx.wbuf[:, 256:512], W.arout_h[g, :, :])
        p.dma("scalar", ctx.wbuf[:, 512:768], W.arout_l[g, :, :])
        for t4 in range(4):
            t0, t1 = t4 * 512, t4 * 512 + 512
            bank = banks.take()
            first = True
            for kc2 in range(2):
                wh = ctx.wbuf[:, 256 + kc2 * 128:256 + kc2 * 128 + 128]
                wl = ctx.wbuf[:, 512 + kc2 * 128:512 + kc2 * 128 + 128]
                lh_ = kxb[:, 8192 + kc2 * 2048 + t0: 8192 + kc2 * 2048 + t1]
                ll_ = kxb[:, 12288 + kc2 * 2048 + t0: 12288 + kc2 * 2048 + t1]
                mk.mm(bank[:, 0:512], wh, lh_, first, False)
                first = False
                mk.mm(bank[:, 0:512], wh, ll_, False, False)
                mk.mm(bank[:, 0:512], wl, lh_, False, kc2 == 1)
            remote_f = ctx.ar_c[:, 0:512]
            mk.ts("vector", remote_f, bank[:, 0:512], sm.col("l1rout_b", g),
                  None, ALU.add)
            local_f = ctx.ar_b[:, 0:512]
            mk.tt("vector", local_f, arab[:, g * 4096 + t0:g * 4096 + t1],
                  arab[:, g * 4096 + 2048 + t0:g * 4096 + 2048 + t1], ALU.add)
            gb = banks.take()
            mk.mm(gb[0:2, 0:512], mk.glpat[:, :], local_f, True, False)
            mk.mm(gb[0:2, 0:512], mk.grpat[:, :], remote_f, False, True)
            grows = ctx.ar_b[0:2, 512:1024]
            mk.act(grows, gb[0:2, 0:512], AF.Sigmoid)
            gbc = banks.take()
            mk.mm(gbc[:, 0:512], mk.pat_2t[0:2, :], grows, True, True)
            diff = ctx.ar_b[:, 1024:1536]
            mk.tt("vector", diff, local_f, remote_f, ALU.subtract)
            mk.tt("vector", diff, diff, gbc[:, 0:512], ALU.mult)
            mixf = ctx.ar_c[:, 512:1024]
            mk.tt("vector", mixf, diff, remote_f, ALU.add)
            mk.act(arab[:, g * 4096 + t0:g * 4096 + t1], mixf, AF.Copy)
            mk.tt("vector", arab[:, g * 4096 + 2048 + t0:g * 4096 + 2048 + t1],
                  mixf, arab[:, g * 4096 + t0:g * 4096 + t1], ALU.subtract)

    mk.tap("aamix", ctx.ar_a[:, :].bitcast(BF16))
    for oc in range(8):
        p.dma("sync", ctx.wbuf[:, 0:1024], W.aout_h[oc, :, :])
        p.dma("scalar", ctx.wbuf[:, 1024:2048], W.aout_l[oc, :, :])
        for t4 in range(4):
            bank = banks.take()
            t0, t1 = t4 * 512, t4 * 512 + 512
            first = True
            for kc in range(NCHUNK):
                wh = ctx.wbuf[:, kc * 128:(kc + 1) * 128]
                wl = ctx.wbuf[:, 1024 + kc * 128:1024 + kc * 128 + 128]
                ah = arab[:, kc * 4096 + t0: kc * 4096 + t1]
                al = arab[:, kc * 4096 + 2048 + t0: kc * 4096 + 2048 + t1]
                mk.mm(bank[:, 0:512], wh, ah, first, False)
                first = False
                mk.mm(bank[:, 0:512], wh, al, False, False)
                mk.mm(bank[:, 0:512], wl, ah, False, kc == NCHUNK - 1)
            t_ = ctx.ar_c[:, 0:512]
            mk.stt("vector", t_, ctx.xh(oc, t0, t1), sm.col("l1out_b", oc),
                   bank[:, 0:512], ALU.add, ALU.add)
            mk.tt("vector", t_, t_, ctx.xl(oc, t0, t1), ALU.add)
            mk.act(ctx.xh(oc, t0, t1), t_, AF.Copy)
            mk.tt("vector", ctx.xl(oc, t0, t1), t_,
                  ctx.xh(oc, t0, t1), ALU.subtract)


BLOCK_EMIT = {}


def build_program():
    nc = bass.Bass("TRN2", target_bir_lowering=False, detect_race_conditions=False)
    mk = MK(nc)
    p = mk.p

    ins = {}
    ins["L0_attn_a_x"] = mk.dram("L0_attn_a_x", [NSEQ, S, D], F32, kind="ExternalInput")
    ins["L0_ffn_packed_weight"] = mk.dram("L0_ffn_packed_weight", [262144], I32,
                                          kind="ExternalInput")
    for nm, shp in (("L0_attn_a_Wqkv", [3072, D]), ("L0_attn_a_bqkv", [3072]),
                    ("L0_attn_a_Wproj", [D, D]), ("L0_attn_a_bproj", [D]),
                    ("L0_attn_a_qn_w", [64]), ("L0_attn_a_kn_w", [64]),
                    ("L1_ffn_w_gate", [4096, D]), ("L1_ffn_w_up", [4096, D]),
                    ("L1_ffn_w_down", [D, 4096]),
                    ("L1_attn_b_in_proj_w", [1040, D]), ("L1_attn_b_conv_w", [D, 1, 4]),
                    ("L1_attn_b_conv_b", [D]), ("L1_attn_b_w_base", [16]),
                    ("L1_attn_b_out_proj_w", [D, D]),
                    ("L1_attn_a_qkv_w", [D, 3072]), ("L1_attn_a_qkv_b", [3072]),
                    ("L1_attn_a_rq_w", [D, 256]), ("L1_attn_a_rq_b", [256]),
                    ("L1_attn_a_rkv_w", [D, 512]), ("L1_attn_a_rkv_b", [512]),
                    ("L1_attn_a_rout_w", [256, D]), ("L1_attn_a_rout_b", [D]),
                    ("L1_attn_a_out_w", [D, D]), ("L1_attn_a_out_b", [D]),
                    ("L1_attn_a_gl_w", [64, 1]), ("L1_attn_a_gr_w", [64, 1]),
                    ("L0_attn_b_wave_w", [16, D]), ("L0_attn_b_wave_b", [16]),
                    ("L0_attn_b_kernel_w", [512, D]), ("L0_attn_b_kernel_b", [512]),
                    ("L0_attn_b_out_w", [D, D])):
        ins[nm] = mk.dram(nm, shp, F32, kind="ExternalInput")
    ins["L0_ffn_weight_scale"] = mk.dram("L0_ffn_weight_scale", [1], F32,
                                         kind="ExternalInput")
    ins["L0_ffn_bias"] = mk.dram("L0_ffn_bias", [D], F32, kind="ExternalInput")
    out = mk.dram("out", [NSEQ, S, D], F32, kind="ExternalOutput")

    mk.build_constants()
    banks = Banks(mk)
    mk.finish_constants(banks)
    ctx = SeqCtx(mk, banks)
    if "l0ab" in BLOCK_SET:
        mk.build_l0ab_constants(banks, ctx)
    sm = Smalls(mk)
    sm.load("l0f_ws", ins["L0_ffn_weight_scale"], 1)
    sm.load("l0f_b", ins["L0_ffn_bias"], D)
    if "l0aa" in BLOCK_SET:
        sm.load("bqkv", ins["L0_attn_a_bqkv"], 3072)
        sm.load("bproj", ins["L0_attn_a_bproj"], D)
        c0 = sm.next
        sm.cols["qnkn"] = (c0, 128)
        sm.next += 1
        for half in range(2):
            p.dma("gpsimd", sm.buf[half * 64:half * 64 + 64, c0:c0 + 1],
                  ins["L0_attn_a_qn_w"].rearrange("(p c) -> p c", c=1), slow=True)
        qn2 = mk.sb("kn2", [128, 1], F32)
        for half in range(2):
            p.dma("gpsimd", qn2[half * 64:half * 64 + 64, 0:1],
                  ins["L0_attn_a_kn_w"].rearrange("(p c) -> p c", c=1), slow=True)
        mk.tt("vector", sm.buf[:, c0:c0 + 1], sm.buf[:, c0:c0 + 1], qn2[:, 0:1],
              ALU.mult)
    if "l1ab" in BLOCK_SET:
        c0 = sm.next
        sm.cols["convw"] = (c0, 32 * 128)
        sm.next += 32
        cw = ins["L1_attn_b_conv_w"].rearrange("(c p) o j -> p c (o j)", p=128)
        p.dma("gpsimd", sm.buf[:, c0:c0 + 32].rearrange("p (c j) -> p c j", j=4),
              cw, slow=True)
        sm.load("convb", ins["L1_attn_b_conv_b"], D)
        sm.load("wbase", ins["L1_attn_b_w_base"], 16)
    if "l0ab" in BLOCK_SET:
        sm.load("wvb", ins["L0_attn_b_wave_b"], 16)
        sm.load("knb", ins["L0_attn_b_kernel_b"], 512)
    if "l1aa" in BLOCK_SET:
        sm.load("l1qkv_b", ins["L1_attn_a_qkv_b"], 3072)
        sm.load("l1rq_b", ins["L1_attn_a_rq_b"], 256)
        sm.load("l1rkv_b", ins["L1_attn_a_rkv_b"], 512)
        sm.load("l1rout_b", ins["L1_attn_a_rout_b"], D)
        sm.load("l1out_b", ins["L1_attn_a_out_b"], D)

    W = Wts()
    emit_prep(mk, W, ctx, ins)

    NTC = S // 128
    out_toks = []
    for seq in range(NSEQ):
        tm = ctx.ar_a
        for tc in range(NTC):
            q = ("sync", "scalar")[tc % 2]
            p.dma(q, tm[:, tc * D:(tc + 1) * D], ins["L0_attn_a_x"][seq, tc * 128:(tc + 1) * 128, :])
        for fc in range(NCHUNK):
            for tc in range(NTC):
                bank = banks.take()
                src = tm[:, tc * D + fc * 128: tc * D + fc * 128 + 128]
                mk.pe_transpose(bank[:, 0:128], src)
                dhi = ctx.xh(fc, tc * 128, tc * 128 + 128)
                dlo = ctx.xl(fc, tc * 128, tc * 128 + 128)
                mk.act(dhi, bank[:, 0:128], AF.Copy)
                mk.tt("vector", dlo, bank[:, 0:128], dhi, ALU.subtract)

        for bname in BLOCKS:
            if bname not in BLOCK_SET:
                continue
            if bname in ("l0ffn", "l0ffn2"):
                emit_l0_ffn(mk, ctx, W, sm, bname == "l0ffn2")
            elif bname == "l0aa":
                emit_l0_attn_a(mk, ctx, W, sm, ins)
            elif bname in ("l1ffn", "l1ffn2"):
                emit_l1_ffn(mk, ctx, W, sm)
            elif bname == "l1ab":
                emit_l1_ab(mk, ctx, W, sm)
            elif bname == "l1aa":
                emit_l1_attn_a(mk, ctx, W, sm, ins)
            elif bname == "l0ab":
                emit_l0_attn_b(mk, ctx, W, sm, ins)

        out_tm = ctx.ar_a
        for fc in range(NCHUNK):
            full = ctx.ar_c[:, 0:S]
            mk.tt("vector", full, ctx.xh(fc), ctx.xl(fc), ALU.add)
            for tc in range(NTC):
                bank = banks.take()
                mk.pe_transpose(bank[:, 0:128], full[:, tc * 128:(tc + 1) * 128])
                dst = out_tm[:, tc * D + fc * 128: tc * D + fc * 128 + 128]
                mk.cp("vector", dst, bank[:, 0:128])
        seq_dma = []
        for tc in range(NTC):
            q = ("sync", "scalar")[tc % 2]
            seq_dma.append(p.dma(q, out[seq, tc * 128:(tc + 1) * 128, :],
                                 out_tm[:, tc * D:(tc + 1) * D]))
        out_toks.append(mk.barrier("gpsimd", seq_dma))

    mk.barrier("gpsimd", out_toks)
    p.finalize()
    p.replay()
    return nc


_NC_CACHE = None


def kernel(**inputs):
    global _NC_CACHE
    if _NC_CACHE is None:
        _NC_CACHE = build_program()
    nc = _NC_CACHE
    x = np.ascontiguousarray(inputs["L0_attn_a_x"])
    shared = {k: np.ascontiguousarray(v) for k, v in inputs.items()
              if k != "L0_attn_a_x" and k in (
                  "L0_ffn_packed_weight", "L0_ffn_weight_scale", "L0_ffn_bias",
                  "L0_attn_a_Wqkv", "L0_attn_a_bqkv", "L0_attn_a_Wproj",
                  "L0_attn_a_bproj", "L0_attn_a_qn_w", "L0_attn_a_kn_w",
                  "L1_ffn_w_gate", "L1_ffn_w_up", "L1_ffn_w_down",
                  "L1_attn_b_in_proj_w", "L1_attn_b_conv_w", "L1_attn_b_conv_b",
                  "L1_attn_b_w_base", "L1_attn_b_out_proj_w",
                  "L1_attn_a_qkv_w", "L1_attn_a_qkv_b", "L1_attn_a_rq_w",
                  "L1_attn_a_rq_b", "L1_attn_a_rkv_w", "L1_attn_a_rkv_b",
                  "L1_attn_a_rout_w", "L1_attn_a_rout_b", "L1_attn_a_out_w",
                  "L1_attn_a_out_b", "L1_attn_a_gl_w", "L1_attn_a_gr_w",
                  "L0_attn_b_wave_w", "L0_attn_b_wave_b", "L0_attn_b_kernel_w",
                  "L0_attn_b_kernel_b", "L0_attn_b_out_w")}
    in_maps = [dict(shared, L0_attn_a_x=x[2 * c:2 * c + 2]) for c in range(8)]
    res = run_bass_kernel_spmd(nc, in_maps, core_ids=list(range(8)))
    outs = [np.asarray(r["out"]) for r in res.results]
    return np.concatenate(outs, axis=0).reshape(16, 2048, 1024).astype(np.float32)
